# version 12
# baseline (speedup 1.0000x reference)
import numpy as np
import concourse.bass as bass
import concourse.mybir as mybir
from concourse.bass_utils import run_bass_kernel_spmd

F32 = mybir.dt.float32
F32R = mybir.dt.float32r
BF16 = mybir.dt.bfloat16
U8 = mybir.dt.uint8
AF = mybir.ActivationFunctionType
ALU = mybir.AluOpType

NT = 4
TT = 512
D = 1024
HID = 4096
NQH = 16
NKV = 4
EPS = 1e-5
SCALE = 0.125

ENGS = ["sync", "gpsimd", "tensor", "vector", "scalar"]
ESEM = {"sync": "sD", "gpsimd": "sW", "tensor": "sT", "vector": "sV",
        "scalar": "sS"}
DMA_ENG = {"sync", "gpsimd"}
DMA_SEMS = ["sD", "sW", "sDx", "sDx2", "sDt", "sDw0", "sDw1", "sDp", "sDo",
            "sWa0", "sWa1", "sW10", "sW11", "sW20", "sW21", "sWm"]
ALL_SEMS = DMA_SEMS + ["sT", "sV", "sS"]


class Buf:
    __slots__ = ("writer", "readers")

    def __init__(self):
        self.writer = None
        self.readers = []


class Sched:
    def __init__(self):
        self.steps = []
        self.count = {s: 0 for s in ALL_SEMS}
        self.stopped = False

    def _waits(self, eng, reads, writes, waits):
        w = {}

        def need(tok):
            if tok is None:
                return
            s, v, xeng = tok
            if s == ESEM[eng] and eng not in DMA_ENG and not xeng:
                return
            if w.get(s, 0) < v:
                w[s] = v
        for b in reads:
            need(b.writer)
        for b in writes:
            need(b.writer)
            for tok in b.readers:
                need(tok)
        for s, v in waits:
            if w.get(s, 0) < v:
                w[s] = v
        return w

    def step(self, eng, fn, reads=(), writes=(), waits=(), n_inst=1,
             dve=False, sem=None):
        sem = sem or ESEM[eng]
        if self.stopped:
            return (sem, self.count[sem], dve)
        w = self._waits(eng, reads, writes, waits)
        amt = 16 if eng in DMA_ENG else 1
        self.count[sem] += amt * n_inst
        tok = (sem, self.count[sem], dve)
        for b in reads:
            b.readers.append(tok)
        for b in writes:
            b.writer = tok
            b.readers = []
        self.steps.append((eng, w, fn, amt, sem))
        return tok

    def accum(self, eng, fn, reads, pbufs, first, last):
        if first:
            tok = self.step(eng, fn, reads=reads, writes=pbufs)
        else:
            tok = self.step(eng, fn, reads=reads)
            for b in pbufs:
                b.readers.append(tok)
        if last:
            for b in pbufs:
                b.writer = tok
        return tok

    def barrier(self):
        if self.stopped:
            return
        snap = dict(self.count)
        for eng in ENGS:
            waits = {s: v for s, v in snap.items()
                     if v > 0 and not (s == ESEM[eng]
                                       and eng not in DMA_ENG)}
            self.steps.append((eng, waits, None, 0, None))


def carve(nc, slab, specs):
    base = nc.lookup_mloc(slab).addr
    out = {}
    for name, shape, dtype, off in specs:
        h = nc.virtual_sbuf_tensor(name, shape, dtype)
        nc._tensor(h.name, shape, dtype, type="SB")
        m = nc.lookup_mloc(h)
        m.allocated = True
        m.addr = base + off
        out[name] = h
    return out


class _Stop(Exception):
    pass


def build_program(debug=(), stop_after=None):
    nc = bass.Bass("TRN2", target_bir_lowering=False)
    g = {}

    def din(name, shape):
        g[name] = nc.dram_tensor(name, shape, F32, kind="ExternalInput").ap()

    din("xT", [D, 2048])
    din("wq", [D, D]); din("wkv", [D, 512])
    din("wo", [D, D])
    din("bq_r", [128, 8]); din("bk_r", [128, 2]); din("bv_r", [128, 2])
    din("bo_r", [128, 8])
    din("w12", [D, 2 * HID])
    din("h1t", [HID, HID]); din("h2t", [HID, HID]); din("w3t", [HID, D])
    din("s1_r", [128, 32]); din("s2_r", [128, 32])
    din("sh1_r", [128, 32]); din("sh2_r", [128, 32]); din("s3_r", [128, 8])
    din("cosh", [128, 2048]); din("cosl", [128, 2048])
    din("sinh", [128, 2048]); din("sinl", [128, 2048])
    din("canv", [128, 896])
    din("ident", [128, 128])
    out_hbm = nc.dram_tensor("out", [D, 2048], F32, kind="ExternalOutput").ap()

    xpark = nc.dram_tensor("xpark", [D, 2048], F32, kind="Internal").ap()

    SLAB = 192000
    from contextlib import ExitStack
    with ExitStack() as stack:
        ec = stack.enter_context
        onesc = ec(nc.sbuf_tensor("onesc", [128, 128], BF16))
        mneg = ec(nc.sbuf_tensor("mneg", [128, 128], BF16))
        idf = ec(nc.sbuf_tensor("idf", [128, 128], F32))
        canvb = ec(nc.sbuf_tensor("canvb", [128, 896], BF16))
        epsr = ec(nc.sbuf_tensor("epsr", [128, 1], F32))
        onesf = ec(nc.sbuf_tensor("onesf", [128, 1], F32))
        onesr = ec(nc.sbuf_tensor("onesr", [128, 1], F32R))
        bq_s = ec(nc.sbuf_tensor("bq_s", [128, 8], F32))
        bk_s = ec(nc.sbuf_tensor("bk_s", [128, 2], F32))
        bv_s = ec(nc.sbuf_tensor("bv_s", [128, 2], F32))
        bo_s = ec(nc.sbuf_tensor("bo_s", [128, 8], F32))
        s1_s = ec(nc.sbuf_tensor("s1_s", [128, 32], F32))
        s2_s = ec(nc.sbuf_tensor("s2_s", [128, 32], F32))
        sh1_s = ec(nc.sbuf_tensor("sh1_s", [128, 32], F32))
        sh2_s = ec(nc.sbuf_tensor("sh2_s", [128, 32], F32))
        s3_s = ec(nc.sbuf_tensor("s3_s", [128, 8], F32))
        slab = ec(nc.sbuf_tensor("slab", [128, SLAB], U8))
        pb = [ec(nc.psum_tensor(f"pb{i}", [128, 512], F32))
              for i in range(8)]
        semh = {s: ec(nc.semaphore(s)) for s in ALL_SEMS}
        block = ec(nc.Block())

        specs = []
        off = [0]

        def reg(name, shape, dtype, at=None):
            o = off[0] if at is None else at
            sz = int(np.prod(shape[1:])) * (2 if dtype == BF16 else 4)
            specs.append((name, shape, dtype, o))
            end = o + ((sz + 31) // 32) * 32
            if at is None:
                off[0] = end
            else:
                off[0] = max(off[0], end)
            return o

        reg("x_sb", [128, 4096], F32)
        reg("lnrows", [1, 2048], BF16)
        reg("lnrowf", [1, 2048], F32)
        reg("spl_h", [128, 512], BF16)
        reg("spl_l", [128, 512], BF16)
        reg("sq_h", [128, 512], BF16)
        reg("sq_l", [128, 512], BF16)
        pA = off[0]
        reg("khdup", [128, 8192], BF16)
        reg("kldup", [128, 8192], BF16)
        reg("vtokh", [128, 4160], BF16)
        reg("vtokl", [128, 4160], BF16)
        reg("cosh_t", [128, 512], F32)
        reg("cosl_t", [128, 512], BF16)
        reg("sinh_t", [128, 512], F32)
        reg("sinl_t", [128, 512], BF16)
        reg("lnh", [128, 4096], BF16)
        reg("lnl", [128, 4096], BF16)
        reg("qpack", [128, 8192], BF16)
        for i in [0, 1, 2, 3, 5]:
            reg(f"stg{i}", [128, 512], F32)
        reg("sc64", [128, 512], BF16)
        stgE_off = []
        for i in range(3):
            reg(f"eh{i}", [128, 512], BF16)
            reg(f"el{i}", [128, 512], BF16)
            stgE_off.append(reg(f"stgE{i}", [128, 512], F32))
        reg("xr0", [128, 512], F32R, at=stgE_off[0])
        reg("xr1", [128, 512], F32R, at=stgE_off[1])
        reg("oth", [128, 4096], BF16)
        reg("otl", [128, 4096], BF16)
        reg("yh", [128, 4096], BF16)
        reg("yl", [128, 4096], BF16)
        reg("rech", [128, 512], BF16)
        reg("recl", [128, 512], BF16)
        reg("wf32_0", [128, 1024], F32)
        for i in range(2):
            reg(f"wh_{i}", [128, 1024], BF16)
        for i in range(2):
            reg(f"wl_{i}", [128, 1024], BF16)
        endA = off[0]
        reg("hh", [128, 16384], BF16, at=pA)
        reg("hl", [128, 16384], BF16, at=pA + 32768)
        reg("g1h", [128, 16384], BF16, at=pA + 65536)
        reg("lnx2h", [128, 4096], BF16, at=pA + 65536)
        reg("lnx2l", [128, 4096], BF16, at=pA + 73728)
        reg("g1l", [128, 16384], BF16, at=pA + 98304)
        reg("y2h", [128, 4096], BF16, at=pA + 131072)
        reg("y2l", [128, 4096], BF16, at=pA + 139264)
        reg("rbB", [128, 512], F32, at=pA + 131072)
        reg("bbB", [128, 512], F32, at=pA + 131072 + 2048)
        reg("rbB2", [128, 512], F32, at=pA + 65536)
        reg("bbB2", [128, 512], F32, at=pA + 65536 + 2048)
        o2 = pA + 147456
        reg("wbA", [128, 1024], BF16, at=o2)
        reg("wbB", [128, 1024], BF16, at=o2 + 2048)
        for i in range(3):
            reg(f"stB{i}", [128, 512], F32, at=o2 + 4096 + i * 2048)
        endB = o2 + 4096 + 3 * 2048
        assert max(endA, endB) <= SLAB, (endA, endB)
        import os as _os
        if _os.environ.get("KLAYOUT"):
            for n, sh, dt, o in specs:
                sz = int(np.prod(sh[1:])) * (2 if dt == BF16 else 4)
                print(f"  {n:10s} off={o:7d} end={o+sz:7d} {sh}")
        T = carve(nc, slab, specs)

        S = Sched()
        B = {name: Buf() for name, _, _, _ in specs}
        for hn in ["wbA0", "wbA1", "wbB0", "wbB1"]:
            B[hn] = Buf()
        CB = {n: [Buf() for _ in range(8)]
              for n in ["lnh", "lnl", "lnx2h", "lnx2l"]}
        XCB = [Buf() for _ in range(8)]
        B["xr0"] = B["stgE0"]
        B["xr1"] = B["stgE1"]
        B["rbB"] = B["bbB"] = B["y2h"]
        B["rbB2"] = B["bbB2"] = B["g1h"]
        B["lnx2h"] = B["lnx2l"] = B["g1h"]
        for extra in ["consts", "xpark", "out"]:
            B[extra] = Buf()
        PB = [Buf() for _ in range(8)]

        dbg_t = {}
        for name, src, tile in debug:
            if src == "xpark":
                shp, dt = [D, 2048], F32
            else:
                hshape = dict((n, s) for n, s, _, _ in specs)[src]
                shp, dt = hshape, F32
            dbg_t[name] = (nc.dram_tensor(name, shp, F32,
                                          kind="ExternalOutput").ap(),
                           src, tile)
            B[name] = Buf()

        def emit_dbg(after_src, t):
            hit_stop = False
            for name, (ap_out, src, tile) in dbg_t.items():
                if src == after_src and tile == t:
                    if src == "xpark":
                        S.step("gpsimd", lambda e, ap_out=ap_out:
                               e.dma_start(out=ap_out, in_=xpark),
                               reads=[B["xpark"]], writes=[B[name]],
                               sem="sWm")
                    else:
                        S.step("gpsimd", lambda e, ap_out=ap_out, src=src:
                               e.dma_start(out=ap_out, in_=T[src][:, :]),
                               reads=[B[src]], writes=[B[name]],
                               sem="sWm")
                    if stop_after == name:
                        hit_stop = True
            if hit_stop:
                S.stopped = True

        cb = B["consts"]

        def init_consts():
            def ld(e):
                insts = [e.dma_start(out=idf[:], in_=g["ident"])]
                for nm, sb in [("bq_r", bq_s), ("bk_r", bk_s),
                               ("bv_r", bv_s), ("bo_r", bo_s),
                               ("s1_r", s1_s), ("s2_r", s2_s),
                               ("sh1_r", sh1_s), ("sh2_r", sh2_s),
                               ("s3_r", s3_s)]:
                    insts.append(e.dma_start(out=sb[:], in_=g[nm]))
                return insts
            S.step("sync", ld, writes=[cb], n_inst=10)
            S.step("gpsimd", lambda e: e.dma_start(out=canvb[:],
                                                   in_=g["canv"]),
                   reads=[cb], writes=[cb], sem="sWm")
            S.step("vector", lambda e: e.memset(onesc[:], 1.0),
                   reads=[cb], writes=[cb])
            S.step("vector", lambda e: e.memset(epsr[:], EPS),
                   reads=[cb], writes=[cb])
            S.step("vector", lambda e: e.memset(onesf[:], 1.0),
                   reads=[cb], writes=[cb])
            S.step("vector", lambda e: e.tensor_copy(onesr[:], onesf[:]),
                   reads=[cb], writes=[cb])
            S.step("scalar", lambda e: e.activation(
                mneg[:], idf[:], AF.Copy, scale=-1e30),
                reads=[cb], writes=[cb])
            S.step("vector", lambda e: e.memset(T["vtokl"][:, :], 0.0),
                   writes=[B["vtokl"]])

            def ones_cols(e):
                last = None
                for hkv in range(NKV):
                    for c in range(16):
                        co = hkv * 1040 + 65 * c + 64
                        last = e.memset(T["vtokh"][:, co:co + 1], 1.0)
                return last
            S.step("vector", ones_cols, writes=[B["vtokh"]])

        def stats_from(srcget, rbn, bbn, Xn, Yn, Zn, sqf):
            for c in range(8):
                hi, lo, f32src, extra = srcget(c)
                rd = list(extra)
                sqfn = [sqf, "stgE2"][c % 2] if sqf == "stg5" else sqf
                sqhn = ["sq_h", "eh1"][c % 2]
                sqln = ["sq_l", "el1"][c % 2]

                def mm_mean(e, hi=hi, lo=lo, c=c):
                    e.matmul(pb[6][:, :], onesc[:, :], hi,
                             start=(c == 0), stop=False)
                    return e.matmul(pb[6][:, :], onesc[:, :], lo,
                                    start=False, stop=(c == 7))
                S.accum("tensor", mm_mean, rd + [cb], [PB[6]],
                        first=(c == 0), last=(c == 7))
                S.step("scalar", lambda e, f32src=f32src, sqfn=sqfn:
                       e.activation(
                    T[sqfn][:, :], f32src, AF.Square),
                    reads=rd, writes=[B[sqfn]])
                S.step("scalar", lambda e, sqfn=sqfn, sqhn=sqhn: e.copy(
                    T[sqhn][:, :], T[sqfn][:, :]),
                    reads=[B[sqfn]], writes=[B[sqhn]])
                S.step("vector", lambda e, sqfn=sqfn, sqhn=sqhn, sqln=sqln:
                       e.tensor_tensor(
                    T[sqln][:, :], T[sqfn][:, :], T[sqhn][:, :],
                    ALU.subtract),
                    reads=[B[sqfn], B[sqhn]], writes=[B[sqln]])

                def mm_sq(e, c=c, sqhn=sqhn, sqln=sqln):
                    e.matmul(pb[7][:, :], onesc[:, :], T[sqhn][:, :],
                             start=(c == 0), stop=False)
                    return e.matmul(pb[7][:, :], onesc[:, :],
                                    T[sqln][:, :], start=False,
                                    stop=(c == 7))
                S.accum("tensor", mm_sq,
                        [B[sqhn], B[sqln], cb], [PB[7]],
                        first=(c == 0), last=(c == 7))
            X, Y, Z = T[Xn], T[Yn], T[Zn]
            bX, bY, bZ = B[Xn], B[Yn], B[Zn]
            brb, bbb = B[rbn], B[bbn]
            S.step("scalar", lambda e: e.activation(
                X[:, 0:512], pb[6][:, :], AF.Identity, scale=1.0 / D),
                reads=[PB[6]], writes=[bX])
            S.step("scalar", lambda e: e.activation(
                Y[:, 0:512], X[:, 0:512], AF.Square),
                reads=[bX], writes=[bY])
            S.step("vector", lambda e: e.scalar_tensor_tensor(
                Z[:, 0:512], pb[7][:, :], 1.0 / D,
                Y[:, 0:512], ALU.mult, ALU.subtract),
                reads=[PB[7], bY], writes=[bZ])
            S.step("scalar", lambda e: e.activation(
                Y[:, 0:512], Z[:, 0:512], AF.Sqrt,
                bias=epsr[:, 0:1]),
                reads=[bZ, cb], writes=[bY])
            S.step("vector", lambda e: e.reciprocal(
                T[rbn][:, :], Y[:, 0:512]),
                reads=[bY], writes=[brb], dve=True)
            S.step("vector", lambda e: e.scalar_tensor_tensor(
                T[bbn][:, :], X[:, 0:512], -1.0, T[rbn][:, :],
                ALU.mult, ALU.mult),
                reads=[bX, brb], writes=[bbb])

        def src_from_f32(tname, sph_, spl_):
            def srcget(c):
                sph = ["spl_h", "eh0"][c % 2]
                spl = ["spl_l", "el0"][c % 2]
                srcb = XCB[c] if tname == "x_sb" else B[tname]
                sl = T[tname][:, c * 512:c * 512 + 512]
                S.step("scalar", lambda e, sph=sph: e.copy(
                    T[sph][:, :], sl),
                    reads=[srcb], writes=[B[sph]])
                S.step("vector", lambda e, sph=sph, spl=spl:
                       e.tensor_tensor(
                    T[spl][:, :], sl, T[sph][:, :], ALU.subtract),
                    reads=[srcb, B[sph]], writes=[B[spl]])
                return (T[sph][:, :], T[spl][:, :], sl,
                        [B[sph], B[spl], srcb])
            return srcget

        def src_from_pair(hname, lname, ftmp):
            def srcget(c):
                hsl = T[hname][:, c * 512:c * 512 + 512]
                lsl = T[lname][:, c * 512:c * 512 + 512]
                ft = [ftmp, "stgE1"][c % 2] if ftmp != "stB0" else ftmp
                S.step("vector", lambda e, ft=ft: e.tensor_tensor(
                    T[ft][:, :], hsl, lsl, ALU.add),
                    reads=[B[hname], B[lname]], writes=[B[ft]])
                return (hsl, lsl, T[ft][:, :],
                        [B[hname], B[lname], B[ft]])
            return srcget

        def affine_split(srcname, dsth, dstl, rbn, bbn):
            cbh = CB.get(dsth); cbl = CB.get(dstl)
            for c in range(8):
                veng = ("gpsimd" if (c in (2, 5) and rbn == "stgE1")
                        else "vector")
                if rbn == "rbB":
                    st0, st1 = "stB1", "stB2"
                elif veng == "gpsimd":
                    st0, st1 = "stg2", "stg3"
                else:
                    st0 = ["stg0", "stg2"][c % 2]
                    st1 = ["stg1", "stg3"][c % 2]
                sl = T[srcname][:, c * 512:c * 512 + 512]
                dsl = slice(c * 512, c * 512 + 512)
                srcb = XCB[c] if srcname == "x_sb" else B[srcname]
                S.step(veng, lambda e, sl=sl, st0=st0: e.tensor_tensor(
                    T[st0][:, :], sl, T[rbn][:, :], ALU.mult),
                    reads=[srcb, B[rbn]], writes=[B[st0]])
                S.step(veng, lambda e, st0=st0, st1=st1: e.tensor_tensor(
                    T[st1][:, :], T[st0][:, :], T[bbn][:, :], ALU.add),
                    reads=[B[st0], B[bbn]], writes=[B[st1]])
                S.step("scalar", lambda e, dsl=dsl, st1=st1: e.copy(
                    T[dsth][:, dsl], T[st1][:, :]),
                    reads=[B[st1]],
                    writes=[B[dsth]] + ([cbh[c]] if cbh else []))
                S.step(veng, lambda e, dsl=dsl, st1=st1: e.tensor_tensor(
                    T[dstl][:, dsl], T[st1][:, :], T[dsth][:, dsl],
                    ALU.subtract),
                    reads=[B[st1], B[dsth]],
                    writes=[B[dstl]] + ([cbl[c]] if cbl else []))

        def affine_residual_f32(srcname):
            for c in range(8):
                sl = T[srcname][:, c * 512:c * 512 + 512]
                xsl = T["x_sb"][:, c * 512:c * 512 + 512]
                S.step("vector", lambda e, sl=sl: e.tensor_tensor(
                    T["stg0"][:, :], sl, pb[4][:, :], ALU.mult),
                    reads=[B[srcname], PB[4]], writes=[B["stg0"]])
                S.step("vector", lambda e: e.tensor_tensor(
                    T["stg1"][:, :], T["stg0"][:, :], pb[5][:, :], ALU.add),
                    reads=[B["stg0"], PB[5]], writes=[B["stg1"]])
                S.step("vector", lambda e, xsl=xsl: e.tensor_tensor(
                    xsl, xsl, T["stg1"][:, :], ALU.add),
                    reads=[B["stg1"]], writes=[B["x_sb"]])

        def affine_residual_pair(hname, lname, ftmp, st1, rbn, bbn,
                                 store_cb=None):
            for c in range(8):
                veng = ("gpsimd" if (c in (2, 5) and rbn == "stgE1")
                        else "vector")
                ft = ftmp if veng == "vector" else "stg2"
                s1 = st1 if veng == "vector" else "stg3"
                hsl = T[hname][:, c * 512:c * 512 + 512]
                lsl = T[lname][:, c * 512:c * 512 + 512]
                xsl = T["x_sb"][:, c * 512:c * 512 + 512]
                S.step(veng, lambda e, hsl=hsl, lsl=lsl, ft=ft:
                       e.tensor_tensor(
                    T[ft][:, :], hsl, lsl, ALU.add),
                    reads=[B[hname], B[lname]], writes=[B[ft]])
                S.step(veng, lambda e, ft=ft, s1=s1: e.tensor_tensor(
                    T[s1][:, :], T[ft][:, :], T[rbn][:, :], ALU.mult),
                    reads=[B[ft], B[rbn]], writes=[B[s1]])
                S.step(veng, lambda e, s1=s1: e.tensor_tensor(
                    T[s1][:, :], T[s1][:, :], T[bbn][:, :], ALU.add),
                    reads=[B[s1], B[bbn]], writes=[B[s1]])
                S.step(veng, lambda e, xsl=xsl, s1=s1: e.tensor_tensor(
                    xsl, xsl, T[s1][:, :], ALU.add),
                    reads=[B[s1]], writes=[XCB[c]])
                if store_cb is not None:
                    store_cb(c)

        def emit_rope(src, dst, rot, tmp):
            quads = [(0, 32, -1.0), (32, 0, 1.0), (64, 96, -1.0),
                     (96, 64, 1.0)]

            def mk_rot(e):
                last = None
                for o_, i_, sg in quads:
                    last = e.tensor_scalar(
                        T[rot][o_:o_ + 32, :], T[src][i_:i_ + 32, :],
                        sg, None, ALU.mult)
                return last
            S.step("vector", mk_rot, reads=[B[src]], writes=[B[rot]])
            tabs = [B["cosh_t"], B["cosl_t"], B["sinh_t"], B["sinl_t"]]
            S.step("vector", lambda e: e.tensor_tensor(
                T[tmp][:, :], T[src][:, :], T["cosh_t"][:, :], ALU.mult),
                reads=[B[src]] + tabs, writes=[B[tmp]])
            S.step("vector", lambda e: e.tensor_tensor(
                T[dst][:, :], T[rot][:, :], T["sinh_t"][:, :], ALU.mult),
                reads=[B[rot]] + tabs, writes=[B[dst]])
            S.step("vector", lambda e: e.tensor_tensor(
                T[dst][:, :], T[dst][:, :], T[tmp][:, :], ALU.add),
                reads=[B[dst], B[tmp]], writes=[B[dst]])
            S.step("vector", lambda e: e.tensor_tensor(
                T[tmp][:, :], T[src][:, :], T["cosl_t"][:, :], ALU.mult),
                reads=[B[src]] + tabs, writes=[B[tmp]])
            S.step("vector", lambda e: e.tensor_tensor(
                T[dst][:, :], T[dst][:, :], T[tmp][:, :], ALU.add),
                reads=[B[dst], B[tmp]], writes=[B[dst]])
            S.step("vector", lambda e: e.tensor_tensor(
                T[tmp][:, :], T[rot][:, :], T["sinl_t"][:, :], ALU.mult),
                reads=[B[rot]] + tabs, writes=[B[tmp]])
            S.step("vector", lambda e: e.tensor_tensor(
                T[dst][:, :], T[dst][:, :], T[tmp][:, :], ALU.add),
                reads=[B[dst], B[tmp]], writes=[B[dst]])

        def load_wrow(proj, k, slot, ncols):
            fb, hb, lb = (B["wf32_0"], B[f"wh_{slot}"],
                          B[f"wl_{slot}"])
            fsl = T["wf32_0"][:, 0:ncols]
            hsl = T[f"wh_{slot}"][:, 0:ncols]
            lsl = T[f"wl_{slot}"][:, 0:ncols]
            S.step("sync", lambda e: e.dma_start(
                out=fsl, in_=g[proj][k * 128:(k + 1) * 128, 0:ncols]),
                writes=[fb], sem="sDw0")
            S.step("gpsimd", lambda e: e.dma_start(
                out=hsl, in_=g[proj][k * 128:(k + 1) * 128, 0:ncols]),
                writes=[hb], sem=f"sWa{slot}")
            S.step("vector", lambda e: e.tensor_tensor(
                lsl, fsl, hsl, ALU.subtract), reads=[fb, hb], writes=[lb])
            return fb, hb, lb

        init_consts()

        for t in range(NT):
            tc = slice(t * TT, t * TT + TT)

            for half in range(2):
                def ld_x(e, src=g["xT"], tc=tc, half=half):
                    return [e.dma_start(
                        out=T["x_sb"][:, c * 512:c * 512 + 512],
                        in_=src[c * 128:(c + 1) * 128, tc])
                        for c in range(half * 4, half * 4 + 4)]
                S.step("sync", ld_x, writes=XCB[half * 4:half * 4 + 4],
                       n_inst=4, sem=["sDx", "sDx2"][half])

            def ld_tabs_h(e, tc=tc):
                return [e.dma_start(out=T[nm + "_t"][:, :],
                                    in_=g[nm][:, tc])
                        for nm in ["cosh", "sinh"]]
            S.step("sync", ld_tabs_h, n_inst=2, sem="sDt",
                   writes=[B["cosh_t"], B["sinh_t"]])

            def ld_tabs_l(e, tc=tc):
                return [e.dma_start(out=T[nm + "_t"][:, :],
                                    in_=g[nm][:, tc])
                        for nm in ["cosl", "sinl"]]
            S.step("gpsimd", ld_tabs_l, n_inst=2, sem="sWm",
                   writes=[B["cosl_t"], B["sinl_t"]])

            stats_from(src_from_f32("x_sb", "spl_h", "spl_l"),
                       "stgE1", "stgE0", "stg5", "stgE2", "stg0", "stg5")
            affine_split("x_sb", "lnh", "lnl", "stgE1", "stgE0")
            emit_dbg("lnh", t)
            emit_dbg("lnl", t)

            for k in range(8):
                slot = k % 2
                _, bh_kv, bl_kv = load_wrow("wkv", k, slot, 512)

                def mm_kv(e, k=k, slot=slot):
                    last = None
                    for m in range(2):
                        wh = T[f"wh_{slot}"][:, m * 128:m * 128 + 128]
                        wl = T[f"wl_{slot}"][:, m * 128:m * 128 + 128]
                        whv = T[f"wh_{slot}"][:, 256 + m * 128:
                                              256 + m * 128 + 128]
                        wlv = T[f"wl_{slot}"][:, 256 + m * 128:
                                              256 + m * 128 + 128]
                        rh = T["lnh"][:, k * 512:k * 512 + 512]
                        rl = T["lnl"][:, k * 512:k * 512 + 512]
                        e.matmul(pb[m][:, :], wh, rh, start=(k == 0),
                                 stop=False)
                        e.matmul(pb[m][:, :], wh, rl, start=False,
                                 stop=False)
                        e.matmul(pb[m][:, :], wl, rh, start=False,
                                 stop=(k == 7))
                        e.matmul(pb[2 + m][:, :], whv, rh, start=(k == 0),
                                 stop=False)
                        e.matmul(pb[2 + m][:, :], whv, rl, start=False,
                                 stop=False)
                        last = e.matmul(pb[2 + m][:, :], wlv, rh,
                                        start=False, stop=(k == 7))
                    return last
                S.accum("tensor", mm_kv,
                        [bh_kv, bl_kv, CB["lnh"][k], CB["lnl"][k]],
                        [PB[0], PB[1], PB[2], PB[3]],
                        first=(k == 0), last=(k == 7))

            col0 = t * TT
            for m in range(2):
                S.step("scalar", lambda e, m=m: e.activation(
                    T["stg0"][:, :], pb[m][:, :], AF.Identity,
                    bias=bk_s[:, m:m + 1]),
                    reads=[PB[m], cb], writes=[B["stg0"]])
                emit_rope("stg0", "stg1", "stg2", "stg3")
                for par in range(2):
                    hkv = 2 * m + par
                    rs = slice(64 * par, 64 * par + 64)
                    ors = slice(64 * (1 - par), 64 * (1 - par) + 64)
                    kc0 = hkv * 2048 + col0
                    S.step("vector", lambda e, rs=rs, kc0=kc0:
                           e.tensor_copy(
                        T["khdup"][rs, kc0:kc0 + TT], T["stg1"][rs, :]),
                        reads=[B["stg1"]], writes=[B["khdup"]])
                    S.step("vector", lambda e, rs=rs, ors=ors, kc0=kc0:
                           e.tensor_copy(
                        T["khdup"][ors, kc0:kc0 + TT],
                        T["khdup"][rs, kc0:kc0 + TT]),
                        reads=[B["khdup"]], writes=[B["khdup"]])
                    S.step("vector", lambda e, rs=rs, kc0=kc0:
                           e.tensor_tensor(
                        T["kldup"][rs, kc0:kc0 + TT], T["stg1"][rs, :],
                        T["khdup"][rs, kc0:kc0 + TT], ALU.subtract),
                        reads=[B["stg1"], B["khdup"]], writes=[B["kldup"]])
                    S.step("vector", lambda e, rs=rs, ors=ors, kc0=kc0:
                           e.tensor_copy(
                        T["kldup"][ors, kc0:kc0 + TT],
                        T["kldup"][rs, kc0:kc0 + TT]),
                        reads=[B["kldup"]], writes=[B["kldup"]])
            emit_dbg("khdup", t)
            emit_dbg("kl2", t)

            for m in range(2):
                S.step("scalar", lambda e, m=m: e.activation(
                    T["stg0"][:, :], pb[2 + m][:, :], AF.Identity,
                    bias=bv_s[:, m:m + 1]),
                    reads=[PB[2 + m], cb], writes=[B["stg0"]])
                for j in range(4):
                    kc = 4 * t + j
                    S.step("tensor", lambda e, j=j: e.transpose(
                        pb[4][:, 0:128],
                        T["stg0"][:, j * 128:j * 128 + 128], idf[:]),
                        reads=[B["stg0"], cb], writes=[PB[4]])
                    for par in range(2):
                        hkv = 2 * m + par
                        csl = slice(64 * par, 64 * par + 64)
                        co = hkv * 1040 + 65 * kc
                        S.step("vector", lambda e, csl=csl, co=co:
                               e.tensor_copy(
                            T["vtokh"][:, co:co + 64], pb[4][:, csl]),
                            reads=[PB[4]], writes=[B["vtokh"]])
                        S.step("vector", lambda e, csl=csl, co=co:
                               e.tensor_tensor(
                            T["vtokl"][:, co:co + 64], pb[4][:, csl],
                            T["vtokh"][:, co:co + 64], ALU.subtract),
                            reads=[PB[4], B["vtokh"]], writes=[B["vtokl"]])
            emit_dbg("vtokh", t)
            emit_dbg("vtokl", t)

            for k in range(8):
                slot = k % 2
                _, bh_q, bl_q = load_wrow("wq", k, slot, 1024)

                def mm_q(e, k=k, slot=slot):
                    last = None
                    for m in range(8):
                        wh = T[f"wh_{slot}"][:, m * 128:m * 128 + 128]
                        wl = T[f"wl_{slot}"][:, m * 128:m * 128 + 128]
                        rh = T["lnh"][:, k * 512:k * 512 + 512]
                        rl = T["lnl"][:, k * 512:k * 512 + 512]
                        e.matmul(pb[m][:, :], wh, rh, start=(k == 0),
                                 stop=False)
                        e.matmul(pb[m][:, :], wh, rl, start=False,
                                 stop=False)
                        last = e.matmul(pb[m][:, :], wl, rh, start=False,
                                        stop=(k == 7))
                    return last
                S.accum("tensor", mm_q,
                        [bh_q, bl_q, CB["lnh"][k], CB["lnl"][k]],
                        [PB[i] for i in range(8)],
                        first=(k == 0), last=(k == 7))

            for c in range(8):
                S.step("scalar", lambda e, c=c: e.activation(
                    T["stg0"][:, :], pb[c][:, :], AF.Identity,
                    bias=bq_s[:, c:c + 1]),
                    reads=[PB[c], cb], writes=[B["stg0"]])
                emit_rope("stg0", "stg1", "stg2", "stg3")
                for par in range(2):
                    j = 2 * c + par
                    qs = slice(j * 512, j * 512 + 512)
                    if par == 0:
                        S.step("vector", lambda e, qs=qs: e.tensor_copy(
                            T["qpack"][0:64, qs], T["stg1"][0:64, :]),
                            reads=[B["stg1"]], writes=[B["qpack"]])
                        S.step("vector", lambda e, qs=qs: e.tensor_tensor(
                            T["qpack"][64:128, qs], T["stg1"][0:64, :],
                            T["qpack"][0:64, qs], ALU.subtract),
                            reads=[B["stg1"], B["qpack"]],
                            writes=[B["qpack"]])
                    else:
                        S.step("vector", lambda e: e.tensor_copy(
                            T["sc64"][64:128, :], T["stg1"][64:128, :]),
                            reads=[B["stg1"]], writes=[B["sc64"]])
                        S.step("vector", lambda e, qs=qs: e.tensor_copy(
                            T["qpack"][0:64, qs], T["sc64"][64:128, :]),
                            reads=[B["sc64"]], writes=[B["qpack"]])
                        S.step("vector", lambda e, qs=qs: e.tensor_tensor(
                            T["qpack"][64:128, qs], T["stg1"][64:128, :],
                            T["sc64"][64:128, :], ALU.subtract),
                            reads=[B["stg1"], B["sc64"]],
                            writes=[B["qpack"]])
            emit_dbg("qpack", t)

            nkc = 4 * t + 4
            LA = 3
            SBANKS = [0, 1, 5, 6]
            EHN = ["eh0", "eh1", "eh2", "spl_h"]
            ELN = ["el0", "el1", "el2", "spl_l"]
            ESN = ["stgE0", "stgE1", "stgE2", None]
            pend = [None]

            def est_ap(ep):
                return (T["wf32_0"][:, 0:512] if ESN[ep] is None
                        else T[ESN[ep]][:, :])

            def est_buf(ep):
                return B["wf32_0"] if ESN[ep] is None else B[ESN[ep]]

            def emit_S(j, hkv, kc):
                sbk = SBANKS[kc % 4]

                def mm_s(e, j=j, hkv=hkv, kc=kc, sbk=sbk, t=t):
                    qs = slice(j * 512, j * 512 + 512)
                    kcc = hkv * 2048 + kc * 128
                    diag = kc >= 4 * t
                    e.matmul(pb[sbk][:, :],
                             T["khdup"][:, kcc:kcc + 128],
                             T["qpack"][:, qs], start=True, stop=False)
                    last = e.matmul(
                        pb[sbk][:, :], T["kldup"][:, kcc:kcc + 128],
                        T["qpack"][:, qs], start=False,
                        stop=(not diag))
                    if diag:
                        dpp = 384 - 128 * (kc - 4 * t)
                        last = e.matmul(pb[sbk][:, :], mneg[:],
                                        canvb[:, dpp:dpp + 512],
                                        start=False, stop=True)
                    return last
                S.step("tensor", mm_s,
                       reads=[B["khdup"], B["kldup"], B["qpack"], cb],
                       writes=[PB[sbk]])
                ep = kc % 4
                S.step("scalar", lambda e, sbk=sbk, ep=ep: e.activation(
                    est_ap(ep), pb[sbk][:, :], AF.Exp,
                    scale=SCALE),
                    reads=[PB[sbk]], writes=[est_buf(ep)])
                S.step("scalar", lambda e, ep=ep: e.copy(
                    T[EHN[ep]][:, :], est_ap(ep)),
                    reads=[est_buf(ep)], writes=[B[EHN[ep]]])
                S.step("gpsimd", lambda e, ep=ep: e.tensor_tensor(
                    T[ELN[ep]][:, :], est_ap(ep),
                    T[EHN[ep]][:, :], ALU.subtract),
                    reads=[est_buf(ep), B[EHN[ep]]],
                    writes=[B[ELN[ep]]])

            def emit_PV(j, hkv, kc, ob):
                ep = kc % 4

                def mm_pv(e, hkv=hkv, kc=kc, ep=ep, ob=ob,
                          first=(kc == 0), lkc=(kc == nkc - 1)):
                    co = hkv * 1040 + 65 * kc
                    vh = T["vtokh"][:, co:co + 65]
                    vl = T["vtokl"][:, co:co + 65]
                    e.matmul(pb[ob][0:65, :], vh, T[EHN[ep]][:, :],
                             start=first, stop=False)
                    e.matmul(pb[ob][0:65, :], vh, T[ELN[ep]][:, :],
                             start=False, stop=False)
                    return e.matmul(pb[ob][0:65, :], vl,
                                    T[EHN[ep]][:, :],
                                    start=False, stop=lkc)
                S.accum("tensor", mm_pv,
                        [B["vtokh"], B["vtokl"], B[EHN[ep]],
                         B[ELN[ep]]],
                        [PB[ob]], first=(kc == 0), last=(kc == nkc - 1))

            def mk_evac(j, ob):
                par = j % 2
                c = j // 2
                ors = slice(64 * par, 64 * par + 64)

                def ev_a():
                    S.step("vector", lambda e, ob=ob: e.reciprocal(
                        T["stg5"][64:65, :], pb[ob][64:65, :]),
                        reads=[PB[ob]], writes=[B["stg5"]], dve=True)
                    S.step("vector", lambda e: e.tensor_copy(
                        T["rech"][64:65, :], T["stg5"][64:65, :]),
                        reads=[B["stg5"]], writes=[B["rech"]])
                    S.step("vector", lambda e: e.tensor_tensor(
                        T["recl"][64:65, :], T["stg5"][64:65, :],
                        T["rech"][64:65, :], ALU.subtract),
                        reads=[B["stg5"], B["rech"]], writes=[B["recl"]])
                    S.step("scalar", lambda e, ob=ob: e.copy(
                        T["stg2"][0:64, :], pb[ob][0:64, :]),
                        reads=[PB[ob]], writes=[B["stg2"]])

                def ev_b():
                    def mm_rec(e):
                        e.matmul(pb[4][0:64, :], onesc[64:65, 0:64],
                                 T["rech"][64:65, :], start=True,
                                 stop=False)
                        return e.matmul(pb[4][0:64, :],
                                        onesc[64:65, 0:64],
                                        T["recl"][64:65, :], start=False,
                                        stop=True)
                    S.step("tensor", mm_rec,
                           reads=[B["rech"], B["recl"], cb],
                           writes=[PB[4]])

                def ev_c():
                    S.step("vector", lambda e, ors=ors: e.tensor_tensor(
                        T["stg3"][ors, :], T["stg2"][0:64, :],
                        pb[4][0:64, :], ALU.mult),
                        reads=[B["stg2"], PB[4]], writes=[B["stg3"]])
                    if par == 1:
                        S.step("scalar", lambda e, c=c: e.copy(
                            T["oth"][:, c * 512:c * 512 + 512],
                            T["stg3"][:, :]),
                            reads=[B["stg3"]], writes=[B["oth"]])
                        S.step("vector", lambda e, c=c: e.tensor_tensor(
                            T["otl"][:, c * 512:c * 512 + 512],
                            T["stg3"][:, :],
                            T["oth"][:, c * 512:c * 512 + 512],
                            ALU.subtract),
                            reads=[B["stg3"], B["oth"]],
                            writes=[B["otl"]])
                return [ev_a, ev_b, ev_c]

            for j in range(NQH):
                hkv = j // 4
                ob = 2 + (j % 2)
                for i in range(nkc + LA):
                    if i < nkc:
                        emit_S(j, hkv, i)
                    if pend[0] is not None and i in (1, 4, 5):
                        pend[0][{1: 0, 4: 1, 5: 2}[i]]()
                        if i == 5:
                            pend[0] = None
                    if i >= LA:
                        emit_PV(j, hkv, i - LA, ob)
                pend[0] = mk_evac(j, ob)
            for ev in pend[0]:
                ev()
            pend[0] = None
            emit_dbg("oth", t)
            emit_dbg("otl", t)

            for k in range(8):
                slot = k % 2
                _, bh_o, bl_o = load_wrow("wo", k, slot, 1024)

                def mm_o(e, k=k, slot=slot):
                    last = None
                    for m in range(8):
                        wh = T[f"wh_{slot}"][:, m * 128:m * 128 + 128]
                        wl = T[f"wl_{slot}"][:, m * 128:m * 128 + 128]
                        rh = T["oth"][:, k * 512:k * 512 + 512]
                        rl = T["otl"][:, k * 512:k * 512 + 512]
                        e.matmul(pb[m][:, :], wh, rh, start=(k == 0),
                                 stop=False)
                        e.matmul(pb[m][:, :], wh, rl, start=False,
                                 stop=False)
                        last = e.matmul(pb[m][:, :], wl, rh, start=False,
                                        stop=(k == 7))
                    return last
                S.accum("tensor", mm_o, [bh_o, bl_o, B["oth"], B["otl"]],
                        [PB[i] for i in range(8)],
                        first=(k == 0), last=(k == 7))
            for c in range(8):
                S.step("scalar", lambda e, c=c: e.activation(
                    T["stg0"][:, :], pb[c][:, :],
                    AF.Identity, bias=bo_s[:, c:c + 1]),
                    reads=[PB[c], cb], writes=[B["stg0"]])
                S.step("scalar", lambda e, c=c: e.copy(
                    T["yh"][:, c * 512:c * 512 + 512], T["stg0"][:, :]),
                    reads=[B["stg0"]], writes=[B["yh"]])
                S.step("vector", lambda e, c=c: e.tensor_tensor(
                    T["yl"][:, c * 512:c * 512 + 512], T["stg0"][:, :],
                    T["yh"][:, c * 512:c * 512 + 512], ALU.subtract),
                    reads=[B["stg0"], B["yh"]], writes=[B["yl"]])
            emit_dbg("yh", t)

            stats_from(src_from_pair("yh", "yl", "stg1"),
                       "stgE1", "stgE0", "stg5", "stgE2", "stg0", "stg5")

            def park_c(c, tc=tc):
                S.step("sync", lambda e, c=c, tc=tc: e.dma_start(
                    out=xpark[c * 128:(c + 1) * 128, tc],
                    in_=T["x_sb"][:, c * 512:c * 512 + 512]),
                    reads=[XCB[c]], writes=[B["xpark"]], sem="sDp")
            affine_residual_pair("yh", "yl", "stg1", "stg0",
                                 "stgE1", "stgE0", store_cb=park_c)
        emit_dbg("xpark", NT - 1)

        S.barrier()
        for t in range(NT):
            tc = slice(t * TT, t * TT + TT)

            for half in range(2):
                def ld_xp(e, tc=tc, half=half):
                    return [e.dma_start(
                        out=T["x_sb"][:, c * 512:c * 512 + 512],
                        in_=xpark[c * 128:(c + 1) * 128, tc])
                        for c in range(half * 4, half * 4 + 4)]
                S.step("sync", ld_xp, reads=[B["xpark"]],
                       writes=XCB[half * 4:half * 4 + 4],
                       n_inst=4, sem=["sDx", "sDx2"][half])

            stats_from(src_from_f32("x_sb", "spl_h", "spl_l"),
                       "rbB", "bbB", "stB0", "stB1", "stB2", "stB1")
            affine_split("x_sb", "lnx2h", "lnx2l", "rbB", "bbB")
            emit_dbg("lnx2h", t)

            for mb in range(8):
                for k in range(8):
                    sn = "wbA" if k % 2 == 0 else "wbB"
                    bwp = [B[sn + "0"], B[sn + "1"]]
                    S.step("gpsimd", lambda e, mb=mb, k=k, sn=sn:
                           e.dma_start(
                        out=T[sn][:, :],
                        in_=g["w12"][k * 128:(k + 1) * 128,
                                     mb * 1024:mb * 1024 + 1024]),
                        writes=bwp, sem=("sW10" if k % 2 == 0 else "sW11"))

                    def mm_w12(e, k=k, sn=sn):
                        last = None
                        for mi in range(4):
                            w1 = T[sn][:, mi * 128:mi * 128 + 128]
                            w2 = T[sn][:, 512 + mi * 128:512 + mi * 128
                                       + 128]
                            rh = T["lnx2h"][:, k * 512:k * 512 + 512]
                            rl = T["lnx2l"][:, k * 512:k * 512 + 512]
                            e.matmul(pb[mi][:, :], w1, rh, start=(k == 0),
                                     stop=False)
                            e.matmul(pb[mi][:, :], w1, rl, start=False,
                                     stop=(k == 7))
                            e.matmul(pb[4 + mi][:, :], w2, rh,
                                     start=(k == 0), stop=False)
                            last = e.matmul(pb[4 + mi][:, :], w2, rl,
                                            start=False, stop=(k == 7))
                        return last
                    S.accum("tensor", mm_w12,
                            bwp + [CB["lnx2h"][k], CB["lnx2l"][k]],
                            [PB[i] for i in range(8)],
                            first=(k == 0), last=(k == 7))
                for mi in range(4):
                    m = mb * 4 + mi
                    S.step("scalar", lambda e, mi=mi, m=m: e.activation(
                        T["stB0"][:, :], pb[mi][:, :], AF.Silu,
                        scale=s1_s[:, m:m + 1]),
                        reads=[PB[mi], cb], writes=[B["stB0"]])
                    S.step("scalar", lambda e, mi=mi, m=m: e.activation(
                        T["stB2"][:, :], pb[4 + mi][:, :], AF.Identity,
                        scale=s2_s[:, m:m + 1]),
                        reads=[PB[4 + mi], cb], writes=[B["stB2"]])
                    S.step("vector", lambda e: e.tensor_tensor(
                        T["stB0"][:, :], T["stB0"][:, :], T["stB2"][:, :],
                        ALU.mult),
                        reads=[B["stB0"], B["stB2"]], writes=[B["stB0"]])
                    S.step("scalar", lambda e, m=m: e.copy(
                        T["hh"][:, m * 512:m * 512 + 512],
                        T["stB0"][:, :]),
                        reads=[B["stB0"]], writes=[B["hh"]])
                    S.step("vector", lambda e, m=m: e.tensor_tensor(
                        T["hl"][:, m * 512:m * 512 + 512], T["stB0"][:, :],
                        T["hh"][:, m * 512:m * 512 + 512], ALU.subtract),
                        reads=[B["stB0"], B["hh"]], writes=[B["hl"]])
            emit_dbg("hh", t)
            emit_dbg("hl", t)

            WSLOT = [("wbA", 0), ("wbA", 512), ("wbB", 0), ("wbB", 512)]
            WSEM = ["sW10", "sW11", "sW20", "sW21"]

            def big_mat(wname, srch, srcl, dsth, dstl, scl_s, func, n_m):
                for mb in range(n_m // 4):
                    bb = 4 * (mb % 2)
                    for k in range(32):
                        sn, co = WSLOT[k % 4]
                        bw = B[sn + ("0" if co == 0 else "1")]
                        S.step("gpsimd", lambda e, mb=mb, k=k, sn=sn,
                               co=co, wname=wname: e.dma_start(
                            out=T[sn][:, co:co + 512],
                            in_=g[wname][k * 128:(k + 1) * 128,
                                         mb * 512:mb * 512 + 512]),
                            writes=[bw], sem=WSEM[k % 4])

                        def mm_big(e, k=k, sn=sn, co=co, bb=bb):
                            last = None
                            for mi in range(4):
                                w = T[sn][:, co + mi * 128:co + mi * 128
                                          + 128]
                                rh = T[srch][:, k * 512:k * 512 + 512]
                                rl = T[srcl][:, k * 512:k * 512 + 512]
                                e.matmul(pb[bb + mi][:, :], w, rh,
                                         start=(k == 0), stop=False)
                                last = e.matmul(pb[bb + mi][:, :], w, rl,
                                                start=False,
                                                stop=(k == 31))
                            return last
                        S.accum("tensor", mm_big,
                                [bw, B[srch], B[srcl]],
                                [PB[bb + mi] for mi in range(4)],
                                first=(k == 0), last=(k == 31))
                    for mi in range(4):
                        m = mb * 4 + mi
                        S.step("scalar", lambda e, bb=bb, mi=mi, m=m:
                               e.activation(
                            T["stB0"][:, :], pb[bb + mi][:, :], func,
                            scale=scl_s[:, m:m + 1]),
                            reads=[PB[bb + mi], cb], writes=[B["stB0"]])
                        S.step("scalar", lambda e, m=m: e.copy(
                            T[dsth][:, m * 512:m * 512 + 512],
                            T["stB0"][:, :]),
                            reads=[B["stB0"]], writes=[B[dsth]])
                        S.step("vector", lambda e, m=m: e.tensor_tensor(
                            T[dstl][:, m * 512:m * 512 + 512],
                            T["stB0"][:, :],
                            T[dsth][:, m * 512:m * 512 + 512],
                            ALU.subtract),
                            reads=[B["stB0"], B[dsth]], writes=[B[dstl]])

            big_mat("h1t", "hh", "hl", "g1h", "g1l", sh1_s, AF.Gelu, 32)
            emit_dbg("g1h", t)
            emit_dbg("g1l", t)
            big_mat("h2t", "g1h", "g1l", "hh", "hl", sh2_s, AF.Gelu, 32)
            big_mat("w3t", "hh", "hl", "y2h", "y2l", s3_s, AF.Identity, 8)
            emit_dbg("y2h", t)
            emit_dbg("y2l", t)

            stats_from(src_from_pair("y2h", "y2l", "stB0"),
                       "rbB2", "bbB2", "stB1", "stB2", "stg5", "stB1")

            def out_c(c, tc=tc):
                S.step("sync", lambda e, c=c, tc=tc: e.dma_start(
                    out=out_hbm[c * 128:(c + 1) * 128, tc],
                    in_=T["x_sb"][:, c * 512:c * 512 + 512]),
                    reads=[XCB[c]], writes=[B["out"]], sem="sDo")
            affine_residual_pair("y2h", "y2l", "stB0", "stB1",
                                 "rbB2", "bbB2", store_cb=out_c)

        S.steps.append(("sync", {sm: S.count[sm] for sm in DMA_SEMS
                                 if S.count[sm] > 0}, None, 0, None))

        per_eng = {e: [] for e in ENGS}
        for e_name, waits, fn, amt, stsem in S.steps:
            per_eng[e_name].append((waits, fn, amt, stsem))

        def make_section(eng_name):
            def section(eng):
                hw = {}
                for waits, fn, amt, stsem in per_eng[eng_name]:
                    for s in sorted(waits):
                        v = waits[s]
                        if hw.get(s, 0) < v:
                            eng.wait_ge(semh[s], v)
                            hw[s] = v
                    if fn is None:
                        continue
                    res = fn(eng)
                    insts = res if isinstance(res, list) else [res]
                    for inst in insts:
                        inst.then_inc(semh[stsem], amt)
            return section

        block.sync(make_section("sync"))
        block.gpsimd(make_section("gpsimd"))
        block.tensor(make_section("tensor"))
        block.vector(make_section("vector"))
        block.scalar(make_section("scalar"))

    return nc


def make_shared(inp):
    f32 = np.float32
    shared = {
        "wq": np.ascontiguousarray(inp["L0_attn_Wq"], dtype=f32),
        "wkv": np.ascontiguousarray(np.concatenate(
            [np.asarray(inp["L0_attn_Wk"], f32),
             np.asarray(inp["L0_attn_Wv"], f32)], axis=1)),
        "wo": np.ascontiguousarray(inp["L0_attn_Wo"], dtype=f32),
        "bq_r": np.ascontiguousarray(
            np.asarray(inp["L0_attn_bq"], f32).reshape(8, 128).T),
        "bk_r": np.ascontiguousarray(
            np.asarray(inp["L0_attn_bk"], f32).reshape(2, 128).T),
        "bv_r": np.ascontiguousarray(
            np.asarray(inp["L0_attn_bv"], f32).reshape(2, 128).T),
        "bo_r": np.ascontiguousarray(
            np.asarray(inp["L0_attn_bo"], f32).reshape(8, 128).T),
        "w12": np.ascontiguousarray(np.concatenate(
            [np.asarray(inp["L0_ffn_w1_w"], f32).T.reshape(1024, 8, 512),
             np.asarray(inp["L0_ffn_w2_w"], f32).T.reshape(1024, 8, 512)],
            axis=2).reshape(1024, 8192)),
        "h1t": np.ascontiguousarray(np.asarray(inp["L0_ffn_h1_w"], f32).T),
        "h2t": np.ascontiguousarray(np.asarray(inp["L0_ffn_h2_w"], f32).T),
        "w3t": np.ascontiguousarray(np.asarray(inp["L0_ffn_w3_w"], f32).T),
        "s1_r": np.ascontiguousarray(
            np.asarray(inp["L0_ffn_w1_s"], f32).reshape(32, 128).T),
        "s2_r": np.ascontiguousarray(
            np.asarray(inp["L0_ffn_w2_s"], f32).reshape(32, 128).T),
        "sh1_r": np.ascontiguousarray(
            np.asarray(inp["L0_ffn_h1_s"], f32).reshape(32, 128).T),
        "sh2_r": np.ascontiguousarray(
            np.asarray(inp["L0_ffn_h2_s"], f32).reshape(32, 128).T),
        "s3_r": np.ascontiguousarray(
            np.asarray(inp["L0_ffn_w3_s"], f32).reshape(8, 128).T),
    }
    invf = 1.0 / (10000.0 ** (np.arange(32, dtype=np.float64) / 32.0))
    pos = np.arange(2048, dtype=np.float64)
    ang = invf[(np.arange(128) % 32)][:, None] * pos[None, :]
    c64, s64 = np.cos(ang), np.sin(ang)
    ch = c64.astype(f32); cl = (c64 - ch).astype(f32)
    sh = s64.astype(f32); sl = (s64 - sh).astype(f32)
    shared.update({"cosh": ch, "cosl": cl, "sinh": sh, "sinl": sl})
    shared["canv"] = (np.arange(128)[:, None]
                      > (np.arange(896)[None, :] - 384)).astype(f32)
    shared["ident"] = np.eye(128, dtype=f32)
    return shared


def kernel(**inp):
    f32 = np.float32
    x = np.asarray(inp["L0_attn_x"], dtype=f32)
    xT = np.ascontiguousarray(np.transpose(x, (0, 2, 1)))
    shared = make_shared(inp)
    nc = build_program()
    in_maps = [dict(shared, xT=np.ascontiguousarray(xT[b]))
               for b in range(8)]
    res = run_bass_kernel_spmd(nc, in_maps, core_ids=list(range(8)))
    out = np.stack([np.asarray(res.results[b]["out"]).T for b in range(8)])
    return np.ascontiguousarray(out.astype(np.float32))


# version 13
# speedup vs baseline: 1.0958x; 1.0958x over previous
import numpy as np
import concourse.bass as bass
import concourse.mybir as mybir
from concourse.bass_utils import run_bass_kernel_spmd

F32 = mybir.dt.float32
F32R = mybir.dt.float32r
BF16 = mybir.dt.bfloat16
U8 = mybir.dt.uint8
AF = mybir.ActivationFunctionType
ALU = mybir.AluOpType

NT = 4
TT = 512
D = 1024
HID = 4096
NQH = 16
NKV = 4
EPS = 1e-5
SCALE = 0.125

ENGS = ["sync", "gpsimd", "tensor", "vector", "scalar"]
ESEM = {"sync": "sD", "gpsimd": "sW", "tensor": "sT", "vector": "sV",
        "scalar": "sS"}
DMA_ENG = {"sync", "gpsimd"}
DMA_SEMS = ["sD", "sW", "sDx", "sDx2", "sDt", "sDw0", "sDw1", "sDp", "sDo",
            "sWa0", "sWa1", "sW10", "sW11", "sW20", "sW21", "sWm"]
ALL_SEMS = DMA_SEMS + ["sT", "sV", "sS"]


class Buf:
    __slots__ = ("writer", "readers")

    def __init__(self):
        self.writer = None
        self.readers = []


class Sched:
    def __init__(self):
        self.steps = []
        self.count = {s: 0 for s in ALL_SEMS}
        self.stopped = False

    def _waits(self, eng, reads, writes, waits):
        w = {}

        def need(tok):
            if tok is None:
                return
            s, v, xeng = tok
            if s == ESEM[eng] and eng not in DMA_ENG and not xeng:
                return
            if w.get(s, 0) < v:
                w[s] = v
        for b in reads:
            need(b.writer)
        for b in writes:
            need(b.writer)
            for tok in b.readers:
                need(tok)
        for s, v in waits:
            if w.get(s, 0) < v:
                w[s] = v
        return w

    def step(self, eng, fn, reads=(), writes=(), waits=(), n_inst=1,
             dve=False, sem=None):
        sem = sem or ESEM[eng]
        if self.stopped:
            return (sem, self.count[sem], dve)
        w = self._waits(eng, reads, writes, waits)
        amt = 16 if eng in DMA_ENG else 1
        self.count[sem] += amt * n_inst
        tok = (sem, self.count[sem], dve)
        for b in reads:
            b.readers.append(tok)
        for b in writes:
            b.writer = tok
            b.readers = []
        self.steps.append((eng, w, fn, amt, sem))
        return tok

    def accum(self, eng, fn, reads, pbufs, first, last):
        if first:
            tok = self.step(eng, fn, reads=reads, writes=pbufs)
        else:
            tok = self.step(eng, fn, reads=reads)
            for b in pbufs:
                b.readers.append(tok)
        if last:
            for b in pbufs:
                b.writer = tok
        return tok

    def barrier(self):
        if self.stopped:
            return
        snap = dict(self.count)
        for eng in ENGS:
            waits = {s: v for s, v in snap.items()
                     if v > 0 and not (s == ESEM[eng]
                                       and eng not in DMA_ENG)}
            self.steps.append((eng, waits, None, 0, None))


def carve(nc, slab, specs):
    base = nc.lookup_mloc(slab).addr
    out = {}
    for name, shape, dtype, off in specs:
        h = nc.virtual_sbuf_tensor(name, shape, dtype)
        nc._tensor(h.name, shape, dtype, type="SB")
        m = nc.lookup_mloc(h)
        m.allocated = True
        m.addr = base + off
        out[name] = h
    return out


class _Stop(Exception):
    pass


def build_program(debug=(), stop_after=None):
    nc = bass.Bass("TRN2", target_bir_lowering=False)
    g = {}

    def din(name, shape):
        g[name] = nc.dram_tensor(name, shape, F32, kind="ExternalInput").ap()

    din("xT", [D, 2048])
    din("wq", [D, D]); din("wkv", [D, 512])
    din("wo", [D, D])
    din("bq_r", [128, 8]); din("bk_r", [128, 2]); din("bv_r", [128, 2])
    din("bo_r", [128, 8])
    din("w12", [D, 2 * HID])
    din("h1t", [HID, HID]); din("h2t", [HID, HID]); din("w3t", [HID, D])
    din("s1_r", [128, 32]); din("s2_r", [128, 32])
    din("sh1_r", [128, 32]); din("sh2_r", [128, 32]); din("s3_r", [128, 8])
    din("cosh", [128, 2048]); din("cosl", [128, 2048])
    din("sinh", [128, 2048]); din("sinl", [128, 2048])
    din("canv", [128, 896])
    din("ident", [128, 128])
    out_hbm = nc.dram_tensor("out", [D, 2048], F32, kind="ExternalOutput").ap()

    xpark = nc.dram_tensor("xpark", [D, 2048], F32, kind="Internal").ap()

    SLAB = 192000
    from contextlib import ExitStack
    with ExitStack() as stack:
        ec = stack.enter_context
        onesc = ec(nc.sbuf_tensor("onesc", [128, 128], BF16))
        mneg = ec(nc.sbuf_tensor("mneg", [128, 128], BF16))
        idf = ec(nc.sbuf_tensor("idf", [128, 128], F32))
        canvb = ec(nc.sbuf_tensor("canvb", [128, 896], BF16))
        epsr = ec(nc.sbuf_tensor("epsr", [128, 1], F32))
        onesf = ec(nc.sbuf_tensor("onesf", [128, 1], F32))
        onesr = ec(nc.sbuf_tensor("onesr", [128, 1], F32R))
        bq_s = ec(nc.sbuf_tensor("bq_s", [128, 8], F32))
        bk_s = ec(nc.sbuf_tensor("bk_s", [128, 2], F32))
        bv_s = ec(nc.sbuf_tensor("bv_s", [128, 2], F32))
        bo_s = ec(nc.sbuf_tensor("bo_s", [128, 8], F32))
        s1_s = ec(nc.sbuf_tensor("s1_s", [128, 32], F32))
        s2_s = ec(nc.sbuf_tensor("s2_s", [128, 32], F32))
        sh1_s = ec(nc.sbuf_tensor("sh1_s", [128, 32], F32))
        sh2_s = ec(nc.sbuf_tensor("sh2_s", [128, 32], F32))
        s3_s = ec(nc.sbuf_tensor("s3_s", [128, 8], F32))
        slab = ec(nc.sbuf_tensor("slab", [128, SLAB], U8))
        pb = [ec(nc.psum_tensor(f"pb{i}", [128, 512], F32))
              for i in range(8)]
        semh = {s: ec(nc.semaphore(s)) for s in ALL_SEMS}
        block = ec(nc.Block())

        specs = []
        off = [0]

        def reg(name, shape, dtype, at=None):
            o = off[0] if at is None else at
            sz = int(np.prod(shape[1:])) * (2 if dtype == BF16 else 4)
            specs.append((name, shape, dtype, o))
            end = o + ((sz + 31) // 32) * 32
            if at is None:
                off[0] = end
            else:
                off[0] = max(off[0], end)
            return o

        reg("x_sb", [128, 4096], F32)
        reg("lnrows", [1, 2048], BF16)
        reg("lnrowf", [1, 2048], F32)
        reg("spl_h", [128, 512], BF16)
        reg("spl_l", [128, 512], BF16)
        reg("sq_h", [128, 512], BF16)
        reg("sq_l", [128, 512], BF16)
        pA = off[0]
        reg("khdup", [128, 8192], BF16)
        reg("kldup", [128, 8192], BF16)
        reg("vtokh", [128, 4160], BF16)
        reg("vtokl", [128, 4160], BF16)
        reg("cosh_t", [128, 512], F32)
        reg("cosl_t", [128, 512], BF16)
        reg("sinh_t", [128, 512], F32)
        reg("sinl_t", [128, 512], BF16)
        reg("lnh", [128, 4096], BF16)
        reg("lnl", [128, 4096], BF16)
        reg("qpack", [128, 8192], BF16)
        for i in [0, 1, 2, 3, 5]:
            reg(f"stg{i}", [128, 512], F32)
        reg("sc64", [128, 512], BF16)
        stgE_off = []
        for i in range(3):
            reg(f"eh{i}", [128, 512], BF16)
            reg(f"el{i}", [128, 512], BF16)
            stgE_off.append(reg(f"stgE{i}", [128, 512], F32))
        reg("xr0", [128, 512], F32R, at=stgE_off[0])
        reg("xr1", [128, 512], F32R, at=stgE_off[1])
        reg("oth", [128, 4096], BF16)
        reg("otl", [128, 4096], BF16)
        reg("yh", [128, 4096], BF16)
        reg("yl", [128, 4096], BF16)
        reg("rech", [128, 512], BF16)
        reg("recl", [128, 512], BF16)
        reg("wf32_0", [128, 1024], F32)
        for i in range(2):
            reg(f"wh_{i}", [128, 1024], BF16)
        for i in range(2):
            reg(f"wl_{i}", [128, 1024], BF16)
        endA = off[0]
        reg("hh", [128, 16384], BF16, at=pA)
        reg("hl", [128, 16384], BF16, at=pA + 32768)
        reg("g1h", [128, 16384], BF16, at=pA + 65536)
        reg("lnx2h", [128, 4096], BF16, at=pA + 65536)
        reg("lnx2l", [128, 4096], BF16, at=pA + 73728)
        reg("g1l", [128, 16384], BF16, at=pA + 98304)
        reg("y2h", [128, 4096], BF16, at=pA + 131072)
        reg("y2l", [128, 4096], BF16, at=pA + 139264)
        reg("rbB", [128, 512], F32, at=pA + 131072)
        reg("bbB", [128, 512], F32, at=pA + 131072 + 2048)
        reg("rbB2", [128, 512], F32, at=pA + 65536)
        reg("bbB2", [128, 512], F32, at=pA + 65536 + 2048)
        o2 = pA + 147456
        reg("wbA", [128, 1024], BF16, at=o2)
        reg("wbB", [128, 1024], BF16, at=o2 + 2048)
        for i in range(3):
            reg(f"stB{i}", [128, 512], F32, at=o2 + 4096 + i * 2048)
        endB = o2 + 4096 + 3 * 2048
        assert max(endA, endB) <= SLAB, (endA, endB)
        import os as _os
        if _os.environ.get("KLAYOUT"):
            for n, sh, dt, o in specs:
                sz = int(np.prod(sh[1:])) * (2 if dt == BF16 else 4)
                print(f"  {n:10s} off={o:7d} end={o+sz:7d} {sh}")
        T = carve(nc, slab, specs)

        S = Sched()
        B = {name: Buf() for name, _, _, _ in specs}
        for hn in ["wbA0", "wbA1", "wbB0", "wbB1"]:
            B[hn] = Buf()
        CB = {n: [Buf() for _ in range(8)]
              for n in ["lnh", "lnl", "lnx2h", "lnx2l"]}
        XCB = [Buf() for _ in range(8)]
        QPB = [Buf() for _ in range(16)]
        KCB = [Buf() for _ in range(4)]
        KLB = [Buf() for _ in range(4)]
        VHB = [Buf() for _ in range(4)]
        VLB = [Buf() for _ in range(4)]
        B["xr0"] = B["stgE0"]
        B["xr1"] = B["stgE1"]
        B["rbB"] = B["bbB"] = B["y2h"]
        B["rbB2"] = B["bbB2"] = B["g1h"]
        B["lnx2h"] = B["lnx2l"] = B["g1h"]
        for extra in ["consts", "xpark", "out"]:
            B[extra] = Buf()
        PB = [Buf() for _ in range(8)]

        dbg_t = {}
        for name, src, tile in debug:
            if src == "xpark":
                shp, dt = [D, 2048], F32
            else:
                hshape = dict((n, s) for n, s, _, _ in specs)[src]
                shp, dt = hshape, F32
            dbg_t[name] = (nc.dram_tensor(name, shp, F32,
                                          kind="ExternalOutput").ap(),
                           src, tile)
            B[name] = Buf()

        def emit_dbg(after_src, t):
            hit_stop = False
            for name, (ap_out, src, tile) in dbg_t.items():
                if src == after_src and tile == t:
                    if src == "xpark":
                        S.step("gpsimd", lambda e, ap_out=ap_out:
                               e.dma_start(out=ap_out, in_=xpark),
                               reads=[B["xpark"]], writes=[B[name]],
                               sem="sWm")
                    else:
                        S.step("gpsimd", lambda e, ap_out=ap_out, src=src:
                               e.dma_start(out=ap_out, in_=T[src][:, :]),
                               reads=[B[src]], writes=[B[name]],
                               sem="sWm")
                    if stop_after == name:
                        hit_stop = True
            if hit_stop:
                S.stopped = True

        cb = B["consts"]

        def init_consts():
            def ld(e):
                insts = [e.dma_start(out=idf[:], in_=g["ident"])]
                for nm, sb in [("bq_r", bq_s), ("bk_r", bk_s),
                               ("bv_r", bv_s), ("bo_r", bo_s),
                               ("s1_r", s1_s), ("s2_r", s2_s),
                               ("sh1_r", sh1_s), ("sh2_r", sh2_s),
                               ("s3_r", s3_s)]:
                    insts.append(e.dma_start(out=sb[:], in_=g[nm]))
                return insts
            S.step("sync", ld, writes=[cb], n_inst=10)
            S.step("gpsimd", lambda e: e.dma_start(out=canvb[:],
                                                   in_=g["canv"]),
                   reads=[cb], writes=[cb], sem="sWm")
            S.step("vector", lambda e: e.memset(onesc[:], 1.0),
                   reads=[cb], writes=[cb])
            S.step("vector", lambda e: e.memset(epsr[:], EPS),
                   reads=[cb], writes=[cb])
            S.step("vector", lambda e: e.memset(onesf[:], 1.0),
                   reads=[cb], writes=[cb])
            S.step("vector", lambda e: e.tensor_copy(onesr[:], onesf[:]),
                   reads=[cb], writes=[cb])
            S.step("scalar", lambda e: e.activation(
                mneg[:], idf[:], AF.Copy, scale=-1e30),
                reads=[cb], writes=[cb])
            S.step("vector", lambda e: e.memset(T["vtokl"][:, :], 0.0),
                   writes=[B["vtokl"]])

            def ones_cols(e):
                last = None
                for hkv in range(NKV):
                    for c in range(16):
                        co = hkv * 1040 + 65 * c + 64
                        last = e.memset(T["vtokh"][:, co:co + 1], 1.0)
                return last
            S.step("vector", ones_cols, writes=[B["vtokh"]])

        def stats_from(srcget, rbn, bbn, Xn, Yn, Zn, sqf):
            for c in range(8):
                hi, lo, f32src, extra = srcget(c)
                rd = list(extra)
                sqfn = [sqf, "stgE2"][c % 2] if sqf == "stg5" else sqf
                sqhn = ["sq_h", "eh1"][c % 2]
                sqln = ["sq_l", "el1"][c % 2]

                def mm_mean(e, hi=hi, lo=lo, c=c):
                    e.matmul(pb[6][:, :], onesc[:, :], hi,
                             start=(c == 0), stop=False)
                    return e.matmul(pb[6][:, :], onesc[:, :], lo,
                                    start=False, stop=(c == 7))
                S.accum("tensor", mm_mean, rd + [cb], [PB[6]],
                        first=(c == 0), last=(c == 7))
                S.step("scalar", lambda e, f32src=f32src, sqfn=sqfn:
                       e.activation(
                    T[sqfn][:, :], f32src, AF.Square),
                    reads=rd, writes=[B[sqfn]])
                S.step("scalar", lambda e, sqfn=sqfn, sqhn=sqhn: e.copy(
                    T[sqhn][:, :], T[sqfn][:, :]),
                    reads=[B[sqfn]], writes=[B[sqhn]])
                S.step("vector", lambda e, sqfn=sqfn, sqhn=sqhn, sqln=sqln:
                       e.tensor_tensor(
                    T[sqln][:, :], T[sqfn][:, :], T[sqhn][:, :],
                    ALU.subtract),
                    reads=[B[sqfn], B[sqhn]], writes=[B[sqln]])

                def mm_sq(e, c=c, sqhn=sqhn, sqln=sqln):
                    e.matmul(pb[7][:, :], onesc[:, :], T[sqhn][:, :],
                             start=(c == 0), stop=False)
                    return e.matmul(pb[7][:, :], onesc[:, :],
                                    T[sqln][:, :], start=False,
                                    stop=(c == 7))
                S.accum("tensor", mm_sq,
                        [B[sqhn], B[sqln], cb], [PB[7]],
                        first=(c == 0), last=(c == 7))
            X, Y, Z = T[Xn], T[Yn], T[Zn]
            bX, bY, bZ = B[Xn], B[Yn], B[Zn]
            brb, bbb = B[rbn], B[bbn]
            S.step("scalar", lambda e: e.activation(
                X[:, 0:512], pb[6][:, :], AF.Identity, scale=1.0 / D),
                reads=[PB[6]], writes=[bX])
            S.step("scalar", lambda e: e.activation(
                Y[:, 0:512], X[:, 0:512], AF.Square),
                reads=[bX], writes=[bY])
            S.step("vector", lambda e: e.scalar_tensor_tensor(
                Z[:, 0:512], pb[7][:, :], 1.0 / D,
                Y[:, 0:512], ALU.mult, ALU.subtract),
                reads=[PB[7], bY], writes=[bZ])
            S.step("scalar", lambda e: e.activation(
                Y[:, 0:512], Z[:, 0:512], AF.Sqrt,
                bias=epsr[:, 0:1]),
                reads=[bZ, cb], writes=[bY])
            S.step("vector", lambda e: e.reciprocal(
                T[rbn][:, :], Y[:, 0:512]),
                reads=[bY], writes=[brb], dve=True)
            S.step("vector", lambda e: e.scalar_tensor_tensor(
                T[bbn][:, :], X[:, 0:512], -1.0, T[rbn][:, :],
                ALU.mult, ALU.mult),
                reads=[bX, brb], writes=[bbb])

        def src_from_f32(tname, sph_, spl_):
            def srcget(c):
                sph = ["spl_h", "eh0"][c % 2]
                spl = ["spl_l", "el0"][c % 2]
                srcb = XCB[c] if tname == "x_sb" else B[tname]
                sl = T[tname][:, c * 512:c * 512 + 512]
                S.step("scalar", lambda e, sph=sph: e.copy(
                    T[sph][:, :], sl),
                    reads=[srcb], writes=[B[sph]])
                S.step("vector", lambda e, sph=sph, spl=spl:
                       e.tensor_tensor(
                    T[spl][:, :], sl, T[sph][:, :], ALU.subtract),
                    reads=[srcb, B[sph]], writes=[B[spl]])
                return (T[sph][:, :], T[spl][:, :], sl,
                        [B[sph], B[spl], srcb])
            return srcget

        def src_from_pair(hname, lname, ftmp):
            def srcget(c):
                hsl = T[hname][:, c * 512:c * 512 + 512]
                lsl = T[lname][:, c * 512:c * 512 + 512]
                ft = [ftmp, "stgE1"][c % 2] if ftmp != "stB0" else ftmp
                S.step("vector", lambda e, ft=ft: e.tensor_tensor(
                    T[ft][:, :], hsl, lsl, ALU.add),
                    reads=[B[hname], B[lname]], writes=[B[ft]])
                return (hsl, lsl, T[ft][:, :],
                        [B[hname], B[lname], B[ft]])
            return srcget

        def affine_split(srcname, dsth, dstl, rbn, bbn):
            cbh = CB.get(dsth); cbl = CB.get(dstl)
            for c in range(8):
                veng = ("gpsimd" if (c in (2, 5) and rbn == "stgE1")
                        else "vector")
                if rbn == "rbB":
                    st0, st1 = "stB1", "stB2"
                elif veng == "gpsimd":
                    st0, st1 = "stg2", "stg3"
                else:
                    st0 = ["stg0", "stg2"][c % 2]
                    st1 = ["stg1", "stg3"][c % 2]
                sl = T[srcname][:, c * 512:c * 512 + 512]
                dsl = slice(c * 512, c * 512 + 512)
                srcb = XCB[c] if srcname == "x_sb" else B[srcname]
                S.step(veng, lambda e, sl=sl, st0=st0: e.tensor_tensor(
                    T[st0][:, :], sl, T[rbn][:, :], ALU.mult),
                    reads=[srcb, B[rbn]], writes=[B[st0]])
                S.step(veng, lambda e, st0=st0, st1=st1: e.tensor_tensor(
                    T[st1][:, :], T[st0][:, :], T[bbn][:, :], ALU.add),
                    reads=[B[st0], B[bbn]], writes=[B[st1]])
                S.step("scalar", lambda e, dsl=dsl, st1=st1: e.copy(
                    T[dsth][:, dsl], T[st1][:, :]),
                    reads=[B[st1]],
                    writes=[B[dsth]] + ([cbh[c]] if cbh else []))
                S.step(veng, lambda e, dsl=dsl, st1=st1: e.tensor_tensor(
                    T[dstl][:, dsl], T[st1][:, :], T[dsth][:, dsl],
                    ALU.subtract),
                    reads=[B[st1], B[dsth]],
                    writes=[B[dstl]] + ([cbl[c]] if cbl else []))

        def affine_residual_f32(srcname):
            for c in range(8):
                sl = T[srcname][:, c * 512:c * 512 + 512]
                xsl = T["x_sb"][:, c * 512:c * 512 + 512]
                S.step("vector", lambda e, sl=sl: e.tensor_tensor(
                    T["stg0"][:, :], sl, pb[4][:, :], ALU.mult),
                    reads=[B[srcname], PB[4]], writes=[B["stg0"]])
                S.step("vector", lambda e: e.tensor_tensor(
                    T["stg1"][:, :], T["stg0"][:, :], pb[5][:, :], ALU.add),
                    reads=[B["stg0"], PB[5]], writes=[B["stg1"]])
                S.step("vector", lambda e, xsl=xsl: e.tensor_tensor(
                    xsl, xsl, T["stg1"][:, :], ALU.add),
                    reads=[B["stg1"]], writes=[B["x_sb"]])

        def affine_residual_pair(hname, lname, ftmp, st1, rbn, bbn,
                                 store_cb=None):
            for c in range(8):
                veng = ("gpsimd" if (c in (2, 5) and rbn == "stgE1")
                        else "vector")
                ft = ftmp if veng == "vector" else "stg2"
                s1 = st1 if veng == "vector" else "stg3"
                hsl = T[hname][:, c * 512:c * 512 + 512]
                lsl = T[lname][:, c * 512:c * 512 + 512]
                xsl = T["x_sb"][:, c * 512:c * 512 + 512]
                S.step(veng, lambda e, hsl=hsl, lsl=lsl, ft=ft:
                       e.tensor_tensor(
                    T[ft][:, :], hsl, lsl, ALU.add),
                    reads=[B[hname], B[lname]], writes=[B[ft]])
                S.step(veng, lambda e, ft=ft, s1=s1: e.tensor_tensor(
                    T[s1][:, :], T[ft][:, :], T[rbn][:, :], ALU.mult),
                    reads=[B[ft], B[rbn]], writes=[B[s1]])
                S.step(veng, lambda e, s1=s1: e.tensor_tensor(
                    T[s1][:, :], T[s1][:, :], T[bbn][:, :], ALU.add),
                    reads=[B[s1], B[bbn]], writes=[B[s1]])
                S.step(veng, lambda e, xsl=xsl, s1=s1: e.tensor_tensor(
                    xsl, xsl, T[s1][:, :], ALU.add),
                    reads=[B[s1]], writes=[XCB[c]])
                if store_cb is not None:
                    store_cb(c)

        def emit_rope(src, dst, rot, tmp):
            quads = [(0, 32, -1.0), (32, 0, 1.0), (64, 96, -1.0),
                     (96, 64, 1.0)]

            def mk_rot(e):
                last = None
                for o_, i_, sg in quads:
                    last = e.tensor_scalar(
                        T[rot][o_:o_ + 32, :], T[src][i_:i_ + 32, :],
                        sg, None, ALU.mult)
                return last
            S.step("vector", mk_rot, reads=[B[src]], writes=[B[rot]])
            tabs = [B["cosh_t"], B["cosl_t"], B["sinh_t"], B["sinl_t"]]
            S.step("vector", lambda e: e.tensor_tensor(
                T[tmp][:, :], T[src][:, :], T["cosh_t"][:, :], ALU.mult),
                reads=[B[src]] + tabs, writes=[B[tmp]])
            S.step("vector", lambda e: e.tensor_tensor(
                T[dst][:, :], T[rot][:, :], T["sinh_t"][:, :], ALU.mult),
                reads=[B[rot]] + tabs, writes=[B[dst]])
            S.step("vector", lambda e: e.tensor_tensor(
                T[dst][:, :], T[dst][:, :], T[tmp][:, :], ALU.add),
                reads=[B[dst], B[tmp]], writes=[B[dst]])
            S.step("vector", lambda e: e.tensor_tensor(
                T[tmp][:, :], T[src][:, :], T["cosl_t"][:, :], ALU.mult),
                reads=[B[src]] + tabs, writes=[B[tmp]])
            S.step("vector", lambda e: e.tensor_tensor(
                T[dst][:, :], T[dst][:, :], T[tmp][:, :], ALU.add),
                reads=[B[dst], B[tmp]], writes=[B[dst]])
            S.step("vector", lambda e: e.tensor_tensor(
                T[tmp][:, :], T[rot][:, :], T["sinl_t"][:, :], ALU.mult),
                reads=[B[rot]] + tabs, writes=[B[tmp]])
            S.step("vector", lambda e: e.tensor_tensor(
                T[dst][:, :], T[dst][:, :], T[tmp][:, :], ALU.add),
                reads=[B[dst], B[tmp]], writes=[B[dst]])

        def load_wrow(proj, k, slot, ncols):
            fb, hb, lb = (B["wf32_0"], B[f"wh_{slot}"],
                          B[f"wl_{slot}"])
            fsl = T["wf32_0"][:, 0:ncols]
            hsl = T[f"wh_{slot}"][:, 0:ncols]
            lsl = T[f"wl_{slot}"][:, 0:ncols]
            S.step("sync", lambda e: e.dma_start(
                out=fsl, in_=g[proj][k * 128:(k + 1) * 128, 0:ncols]),
                writes=[fb], sem="sDw0")
            S.step("gpsimd", lambda e: e.dma_start(
                out=hsl, in_=g[proj][k * 128:(k + 1) * 128, 0:ncols]),
                writes=[hb], sem=f"sWa{slot}")
            S.step("vector", lambda e: e.tensor_tensor(
                lsl, fsl, hsl, ALU.subtract), reads=[fb, hb], writes=[lb])
            return fb, hb, lb

        init_consts()

        for t in range(NT):
            tc = slice(t * TT, t * TT + TT)

            for half in range(2):
                def ld_x(e, src=g["xT"], tc=tc, half=half):
                    return [e.dma_start(
                        out=T["x_sb"][:, c * 512:c * 512 + 512],
                        in_=src[c * 128:(c + 1) * 128, tc])
                        for c in range(half * 4, half * 4 + 4)]
                S.step("sync", ld_x, writes=XCB[half * 4:half * 4 + 4],
                       n_inst=4, sem=["sDx", "sDx2"][half])

            def ld_tabs_h(e, tc=tc):
                return [e.dma_start(out=T[nm + "_t"][:, :],
                                    in_=g[nm][:, tc])
                        for nm in ["cosh", "sinh"]]
            S.step("sync", ld_tabs_h, n_inst=2, sem="sDt",
                   writes=[B["cosh_t"], B["sinh_t"]])

            def ld_tabs_l(e, tc=tc):
                return [e.dma_start(out=T[nm + "_t"][:, :],
                                    in_=g[nm][:, tc])
                        for nm in ["cosl", "sinl"]]
            S.step("gpsimd", ld_tabs_l, n_inst=2, sem="sWm",
                   writes=[B["cosl_t"], B["sinl_t"]])

            stats_from(src_from_f32("x_sb", "spl_h", "spl_l"),
                       "stgE1", "stgE0", "stg5", "stgE2", "stg0", "stg5")
            affine_split("x_sb", "lnh", "lnl", "stgE1", "stgE0")
            emit_dbg("lnh", t)
            emit_dbg("lnl", t)

            for k in range(8):
                slot = k % 2
                _, bh_kv, bl_kv = load_wrow("wkv", k, slot, 512)

                def mm_kv(e, k=k, slot=slot):
                    last = None
                    for m in range(2):
                        wh = T[f"wh_{slot}"][:, m * 128:m * 128 + 128]
                        wl = T[f"wl_{slot}"][:, m * 128:m * 128 + 128]
                        whv = T[f"wh_{slot}"][:, 256 + m * 128:
                                              256 + m * 128 + 128]
                        wlv = T[f"wl_{slot}"][:, 256 + m * 128:
                                              256 + m * 128 + 128]
                        rh = T["lnh"][:, k * 512:k * 512 + 512]
                        rl = T["lnl"][:, k * 512:k * 512 + 512]
                        e.matmul(pb[m][:, :], wh, rh, start=(k == 0),
                                 stop=False)
                        e.matmul(pb[m][:, :], wh, rl, start=False,
                                 stop=False)
                        e.matmul(pb[m][:, :], wl, rh, start=False,
                                 stop=(k == 7))
                        e.matmul(pb[2 + m][:, :], whv, rh, start=(k == 0),
                                 stop=False)
                        e.matmul(pb[2 + m][:, :], whv, rl, start=False,
                                 stop=False)
                        last = e.matmul(pb[2 + m][:, :], wlv, rh,
                                        start=False, stop=(k == 7))
                    return last
                S.accum("tensor", mm_kv,
                        [bh_kv, bl_kv, CB["lnh"][k], CB["lnl"][k]],
                        [PB[0], PB[1], PB[2], PB[3]],
                        first=(k == 0), last=(k == 7))

            col0 = t * TT
            for m in range(2):
                S.step("scalar", lambda e, m=m: e.activation(
                    T["stg0"][:, :], pb[m][:, :], AF.Identity,
                    bias=bk_s[:, m:m + 1]),
                    reads=[PB[m], cb], writes=[B["stg0"]])
                emit_rope("stg0", "stg1", "stg2", "stg3")
                for par in range(2):
                    hkv = 2 * m + par
                    rs = slice(64 * par, 64 * par + 64)
                    ors = slice(64 * (1 - par), 64 * (1 - par) + 64)
                    kc0 = hkv * 2048 + col0
                    S.step("vector", lambda e, rs=rs, kc0=kc0:
                           e.tensor_copy(
                        T["khdup"][rs, kc0:kc0 + TT], T["stg1"][rs, :]),
                        reads=[B["stg1"]], writes=[B["khdup"], KCB[hkv]])
                    S.step("vector", lambda e, rs=rs, ors=ors, kc0=kc0:
                           e.tensor_copy(
                        T["khdup"][ors, kc0:kc0 + TT],
                        T["khdup"][rs, kc0:kc0 + TT]),
                        reads=[B["khdup"]], writes=[B["khdup"], KCB[hkv]])
                    S.step("vector", lambda e, rs=rs, kc0=kc0:
                           e.tensor_tensor(
                        T["kldup"][rs, kc0:kc0 + TT], T["stg1"][rs, :],
                        T["khdup"][rs, kc0:kc0 + TT], ALU.subtract),
                        reads=[B["stg1"], B["khdup"]],
                        writes=[B["kldup"], KLB[hkv]])
                    S.step("vector", lambda e, rs=rs, ors=ors, kc0=kc0:
                           e.tensor_copy(
                        T["kldup"][ors, kc0:kc0 + TT],
                        T["kldup"][rs, kc0:kc0 + TT]),
                        reads=[B["kldup"]], writes=[B["kldup"], KLB[hkv]])
            emit_dbg("khdup", t)
            emit_dbg("kl2", t)

            for m in range(2):
                S.step("scalar", lambda e, m=m: e.activation(
                    T["stg0"][:, :], pb[2 + m][:, :], AF.Identity,
                    bias=bv_s[:, m:m + 1]),
                    reads=[PB[2 + m], cb], writes=[B["stg0"]])
                for j in range(4):
                    kc = 4 * t + j
                    S.step("tensor", lambda e, j=j: e.transpose(
                        pb[4][:, 0:128],
                        T["stg0"][:, j * 128:j * 128 + 128], idf[:]),
                        reads=[B["stg0"], cb], writes=[PB[4]])
                    for par in range(2):
                        hkv = 2 * m + par
                        csl = slice(64 * par, 64 * par + 64)
                        co = hkv * 1040 + 65 * kc
                        S.step("vector", lambda e, csl=csl, co=co:
                               e.tensor_copy(
                            T["vtokh"][:, co:co + 64], pb[4][:, csl]),
                            reads=[PB[4]], writes=[B["vtokh"], VHB[hkv]])
                        S.step("vector", lambda e, csl=csl, co=co:
                               e.tensor_tensor(
                            T["vtokl"][:, co:co + 64], pb[4][:, csl],
                            T["vtokh"][:, co:co + 64], ALU.subtract),
                            reads=[PB[4], B["vtokh"]],
                            writes=[B["vtokl"], VLB[hkv]])
            emit_dbg("vtokh", t)
            emit_dbg("vtokl", t)

            for k in range(8):
                slot = k % 2
                _, bh_q, bl_q = load_wrow("wq", k, slot, 1024)

                def mm_q(e, k=k, slot=slot):
                    last = None
                    for m in range(8):
                        wh = T[f"wh_{slot}"][:, m * 128:m * 128 + 128]
                        wl = T[f"wl_{slot}"][:, m * 128:m * 128 + 128]
                        rh = T["lnh"][:, k * 512:k * 512 + 512]
                        rl = T["lnl"][:, k * 512:k * 512 + 512]
                        e.matmul(pb[m][:, :], wh, rh, start=(k == 0),
                                 stop=False)
                        e.matmul(pb[m][:, :], wh, rl, start=False,
                                 stop=False)
                        last = e.matmul(pb[m][:, :], wl, rh, start=False,
                                        stop=(k == 7))
                    return last
                S.accum("tensor", mm_q,
                        [bh_q, bl_q, CB["lnh"][k], CB["lnl"][k]],
                        [PB[i] for i in range(8)],
                        first=(k == 0), last=(k == 7))

            for c in range(8):
                S.step("scalar", lambda e, c=c: e.activation(
                    T["stg0"][:, :], pb[c][:, :], AF.Identity,
                    bias=bq_s[:, c:c + 1]),
                    reads=[PB[c], cb], writes=[B["stg0"]])
                emit_rope("stg0", "stg1", "stg2", "stg3")
                for par in range(2):
                    j = 2 * c + par
                    qs = slice(j * 512, j * 512 + 512)
                    if par == 0:
                        S.step("vector", lambda e, qs=qs: e.tensor_copy(
                            T["qpack"][0:64, qs], T["stg1"][0:64, :]),
                            reads=[B["stg1"]], writes=[B["qpack"], QPB[j]])
                        S.step("vector", lambda e, qs=qs: e.tensor_tensor(
                            T["qpack"][64:128, qs], T["stg1"][0:64, :],
                            T["qpack"][0:64, qs], ALU.subtract),
                            reads=[B["stg1"], B["qpack"]],
                            writes=[B["qpack"], QPB[j]])
                    else:
                        S.step("vector", lambda e: e.tensor_copy(
                            T["sc64"][64:128, :], T["stg1"][64:128, :]),
                            reads=[B["stg1"]], writes=[B["sc64"]])
                        S.step("vector", lambda e, qs=qs: e.tensor_copy(
                            T["qpack"][0:64, qs], T["sc64"][64:128, :]),
                            reads=[B["sc64"]], writes=[B["qpack"], QPB[j]])
                        S.step("vector", lambda e, qs=qs: e.tensor_tensor(
                            T["qpack"][64:128, qs], T["stg1"][64:128, :],
                            T["sc64"][64:128, :], ALU.subtract),
                            reads=[B["stg1"], B["sc64"]],
                            writes=[B["qpack"], QPB[j]])
            emit_dbg("qpack", t)

            nkc = 4 * t + 4
            LA = 3
            SBANKS = [0, 1, 5, 6]
            EHN = ["eh0", "eh1", "eh2", "spl_h"]
            ELN = ["el0", "el1", "el2", "spl_l"]
            ESN = ["stgE0", "stgE1", "stgE2", None]
            pend = [None]

            def est_ap(ep):
                return (T["wf32_0"][:, 0:512] if ESN[ep] is None
                        else T[ESN[ep]][:, :])

            def est_buf(ep):
                return B["wf32_0"] if ESN[ep] is None else B[ESN[ep]]

            def emit_S(j, hkv, kc):
                sbk = SBANKS[kc % 4]

                def mm_s(e, j=j, hkv=hkv, kc=kc, sbk=sbk, t=t):
                    qs = slice(j * 512, j * 512 + 512)
                    kcc = hkv * 2048 + kc * 128
                    diag = kc >= 4 * t
                    e.matmul(pb[sbk][:, :],
                             T["khdup"][:, kcc:kcc + 128],
                             T["qpack"][:, qs], start=True, stop=False)
                    last = e.matmul(
                        pb[sbk][:, :], T["kldup"][:, kcc:kcc + 128],
                        T["qpack"][:, qs], start=False,
                        stop=(not diag))
                    if diag:
                        dpp = 384 - 128 * (kc - 4 * t)
                        last = e.matmul(pb[sbk][:, :], mneg[:],
                                        canvb[:, dpp:dpp + 512],
                                        start=False, stop=True)
                    return last
                S.step("tensor", mm_s,
                       reads=[KCB[hkv], KLB[hkv], QPB[j], cb],
                       writes=[PB[sbk]])
                ep = kc % 4
                S.step("scalar", lambda e, sbk=sbk, ep=ep: e.activation(
                    est_ap(ep), pb[sbk][:, :], AF.Exp,
                    scale=SCALE),
                    reads=[PB[sbk]], writes=[est_buf(ep)])
                S.step("scalar", lambda e, ep=ep: e.copy(
                    T[EHN[ep]][:, :], est_ap(ep)),
                    reads=[est_buf(ep)], writes=[B[EHN[ep]]])
                S.step("gpsimd", lambda e, ep=ep: e.tensor_tensor(
                    T[ELN[ep]][:, :], est_ap(ep),
                    T[EHN[ep]][:, :], ALU.subtract),
                    reads=[est_buf(ep), B[EHN[ep]]],
                    writes=[B[ELN[ep]]])

            def emit_PV(j, hkv, kc, ob):
                ep = kc % 4

                def mm_pv(e, hkv=hkv, kc=kc, ep=ep, ob=ob,
                          first=(kc == 0), lkc=(kc == nkc - 1)):
                    co = hkv * 1040 + 65 * kc
                    vh = T["vtokh"][:, co:co + 65]
                    vl = T["vtokl"][:, co:co + 65]
                    e.matmul(pb[ob][0:65, :], vh, T[EHN[ep]][:, :],
                             start=first, stop=False)
                    e.matmul(pb[ob][0:65, :], vh, T[ELN[ep]][:, :],
                             start=False, stop=False)
                    return e.matmul(pb[ob][0:65, :], vl,
                                    T[EHN[ep]][:, :],
                                    start=False, stop=lkc)
                S.accum("tensor", mm_pv,
                        [VHB[hkv], VLB[hkv], B[EHN[ep]],
                         B[ELN[ep]]],
                        [PB[ob]], first=(kc == 0), last=(kc == nkc - 1))

            def mk_evac(j, ob):
                par = j % 2
                c = j // 2
                ors = slice(64 * par, 64 * par + 64)

                def ev_a():
                    S.step("vector", lambda e, ob=ob: e.reciprocal(
                        T["stg5"][64:65, :], pb[ob][64:65, :]),
                        reads=[PB[ob]], writes=[B["stg5"]], dve=True)
                    S.step("vector", lambda e: e.tensor_copy(
                        T["rech"][64:65, :], T["stg5"][64:65, :]),
                        reads=[B["stg5"]], writes=[B["rech"]])
                    S.step("vector", lambda e: e.tensor_tensor(
                        T["recl"][64:65, :], T["stg5"][64:65, :],
                        T["rech"][64:65, :], ALU.subtract),
                        reads=[B["stg5"], B["rech"]], writes=[B["recl"]])
                    S.step("scalar", lambda e, ob=ob: e.copy(
                        T["stg2"][0:64, :], pb[ob][0:64, :]),
                        reads=[PB[ob]], writes=[B["stg2"]])

                def ev_b():
                    def mm_rec(e):
                        e.matmul(pb[4][0:64, :], onesc[64:65, 0:64],
                                 T["rech"][64:65, :], start=True,
                                 stop=False)
                        return e.matmul(pb[4][0:64, :],
                                        onesc[64:65, 0:64],
                                        T["recl"][64:65, :], start=False,
                                        stop=True)
                    S.step("tensor", mm_rec,
                           reads=[B["rech"], B["recl"], cb],
                           writes=[PB[4]])

                def ev_c():
                    S.step("vector", lambda e, ors=ors: e.tensor_tensor(
                        T["stg3"][ors, :], T["stg2"][0:64, :],
                        pb[4][0:64, :], ALU.mult),
                        reads=[B["stg2"], PB[4]], writes=[B["stg3"]])
                    if par == 1:
                        S.step("scalar", lambda e, c=c: e.copy(
                            T["oth"][:, c * 512:c * 512 + 512],
                            T["stg3"][:, :]),
                            reads=[B["stg3"]], writes=[B["oth"]])
                        S.step("vector", lambda e, c=c: e.tensor_tensor(
                            T["otl"][:, c * 512:c * 512 + 512],
                            T["stg3"][:, :],
                            T["oth"][:, c * 512:c * 512 + 512],
                            ALU.subtract),
                            reads=[B["stg3"], B["oth"]],
                            writes=[B["otl"]])
                return [ev_a, ev_b, ev_c]

            for j in range(NQH):
                hkv = j // 4
                ob = 2 + (j % 2)
                for i in range(nkc + LA):
                    if i < nkc:
                        emit_S(j, hkv, i)
                    if pend[0] is not None and i in (1, 4, 5):
                        pend[0][{1: 0, 4: 1, 5: 2}[i]]()
                        if i == 5:
                            pend[0] = None
                    if i >= LA:
                        emit_PV(j, hkv, i - LA, ob)
                pend[0] = mk_evac(j, ob)
            for ev in pend[0]:
                ev()
            pend[0] = None
            emit_dbg("oth", t)
            emit_dbg("otl", t)

            for k in range(8):
                slot = k % 2
                _, bh_o, bl_o = load_wrow("wo", k, slot, 1024)

                def mm_o(e, k=k, slot=slot):
                    last = None
                    for m in range(8):
                        wh = T[f"wh_{slot}"][:, m * 128:m * 128 + 128]
                        wl = T[f"wl_{slot}"][:, m * 128:m * 128 + 128]
                        rh = T["oth"][:, k * 512:k * 512 + 512]
                        rl = T["otl"][:, k * 512:k * 512 + 512]
                        e.matmul(pb[m][:, :], wh, rh, start=(k == 0),
                                 stop=False)
                        e.matmul(pb[m][:, :], wh, rl, start=False,
                                 stop=False)
                        last = e.matmul(pb[m][:, :], wl, rh, start=False,
                                        stop=(k == 7))
                    return last
                S.accum("tensor", mm_o, [bh_o, bl_o, B["oth"], B["otl"]],
                        [PB[i] for i in range(8)],
                        first=(k == 0), last=(k == 7))
            for c in range(8):
                S.step("scalar", lambda e, c=c: e.activation(
                    T["stg0"][:, :], pb[c][:, :],
                    AF.Identity, bias=bo_s[:, c:c + 1]),
                    reads=[PB[c], cb], writes=[B["stg0"]])
                S.step("scalar", lambda e, c=c: e.copy(
                    T["yh"][:, c * 512:c * 512 + 512], T["stg0"][:, :]),
                    reads=[B["stg0"]], writes=[B["yh"]])
                S.step("vector", lambda e, c=c: e.tensor_tensor(
                    T["yl"][:, c * 512:c * 512 + 512], T["stg0"][:, :],
                    T["yh"][:, c * 512:c * 512 + 512], ALU.subtract),
                    reads=[B["stg0"], B["yh"]], writes=[B["yl"]])
            emit_dbg("yh", t)

            stats_from(src_from_pair("yh", "yl", "stg1"),
                       "stgE1", "stgE0", "stg5", "stgE2", "stg0", "stg5")

            def park_c(c, tc=tc):
                S.step("sync", lambda e, c=c, tc=tc: e.dma_start(
                    out=xpark[c * 128:(c + 1) * 128, tc],
                    in_=T["x_sb"][:, c * 512:c * 512 + 512]),
                    reads=[XCB[c]], writes=[B["xpark"]], sem="sDp")
            affine_residual_pair("yh", "yl", "stg1", "stg0",
                                 "stgE1", "stgE0", store_cb=park_c)
        emit_dbg("xpark", NT - 1)

        S.barrier()
        for t in range(NT):
            tc = slice(t * TT, t * TT + TT)

            for half in range(2):
                def ld_xp(e, tc=tc, half=half):
                    return [e.dma_start(
                        out=T["x_sb"][:, c * 512:c * 512 + 512],
                        in_=xpark[c * 128:(c + 1) * 128, tc])
                        for c in range(half * 4, half * 4 + 4)]
                S.step("sync", ld_xp, reads=[B["xpark"]],
                       writes=XCB[half * 4:half * 4 + 4],
                       n_inst=4, sem=["sDx", "sDx2"][half])

            stats_from(src_from_f32("x_sb", "spl_h", "spl_l"),
                       "rbB", "bbB", "stB0", "stB1", "stB2", "stB1")
            affine_split("x_sb", "lnx2h", "lnx2l", "rbB", "bbB")
            emit_dbg("lnx2h", t)

            for mb in range(8):
                for k in range(8):
                    sn = "wbA" if k % 2 == 0 else "wbB"
                    bwp = [B[sn + "0"], B[sn + "1"]]
                    S.step("gpsimd", lambda e, mb=mb, k=k, sn=sn:
                           e.dma_start(
                        out=T[sn][:, :],
                        in_=g["w12"][k * 128:(k + 1) * 128,
                                     mb * 1024:mb * 1024 + 1024]),
                        writes=bwp, sem=("sW10" if k % 2 == 0 else "sW11"))

                    def mm_w12(e, k=k, sn=sn):
                        last = None
                        for mi in range(4):
                            w1 = T[sn][:, mi * 128:mi * 128 + 128]
                            w2 = T[sn][:, 512 + mi * 128:512 + mi * 128
                                       + 128]
                            rh = T["lnx2h"][:, k * 512:k * 512 + 512]
                            rl = T["lnx2l"][:, k * 512:k * 512 + 512]
                            e.matmul(pb[mi][:, :], w1, rh, start=(k == 0),
                                     stop=False)
                            e.matmul(pb[mi][:, :], w1, rl, start=False,
                                     stop=(k == 7))
                            e.matmul(pb[4 + mi][:, :], w2, rh,
                                     start=(k == 0), stop=False)
                            last = e.matmul(pb[4 + mi][:, :], w2, rl,
                                            start=False, stop=(k == 7))
                        return last
                    S.accum("tensor", mm_w12,
                            bwp + [CB["lnx2h"][k], CB["lnx2l"][k]],
                            [PB[i] for i in range(8)],
                            first=(k == 0), last=(k == 7))
                for mi in range(4):
                    m = mb * 4 + mi
                    S.step("scalar", lambda e, mi=mi, m=m: e.activation(
                        T["stB0"][:, :], pb[mi][:, :], AF.Silu,
                        scale=s1_s[:, m:m + 1]),
                        reads=[PB[mi], cb], writes=[B["stB0"]])
                    S.step("scalar", lambda e, mi=mi, m=m: e.activation(
                        T["stB2"][:, :], pb[4 + mi][:, :], AF.Identity,
                        scale=s2_s[:, m:m + 1]),
                        reads=[PB[4 + mi], cb], writes=[B["stB2"]])
                    S.step("vector", lambda e: e.tensor_tensor(
                        T["stB0"][:, :], T["stB0"][:, :], T["stB2"][:, :],
                        ALU.mult),
                        reads=[B["stB0"], B["stB2"]], writes=[B["stB0"]])
                    S.step("scalar", lambda e, m=m: e.copy(
                        T["hh"][:, m * 512:m * 512 + 512],
                        T["stB0"][:, :]),
                        reads=[B["stB0"]], writes=[B["hh"]])
                    S.step("vector", lambda e, m=m: e.tensor_tensor(
                        T["hl"][:, m * 512:m * 512 + 512], T["stB0"][:, :],
                        T["hh"][:, m * 512:m * 512 + 512], ALU.subtract),
                        reads=[B["stB0"], B["hh"]], writes=[B["hl"]])
            emit_dbg("hh", t)
            emit_dbg("hl", t)

            WSLOT = [("wbA", 0), ("wbA", 512), ("wbB", 0), ("wbB", 512)]
            WSEM = ["sW10", "sW11", "sW20", "sW21"]

            def big_mat(wname, srch, srcl, dsth, dstl, scl_s, func, n_m):
                for mb in range(n_m // 4):
                    bb = 4 * (mb % 2)
                    for k in range(32):
                        sn, co = WSLOT[k % 4]
                        bw = B[sn + ("0" if co == 0 else "1")]
                        S.step("gpsimd", lambda e, mb=mb, k=k, sn=sn,
                               co=co, wname=wname: e.dma_start(
                            out=T[sn][:, co:co + 512],
                            in_=g[wname][k * 128:(k + 1) * 128,
                                         mb * 512:mb * 512 + 512]),
                            writes=[bw], sem=WSEM[k % 4])

                        def mm_big(e, k=k, sn=sn, co=co, bb=bb):
                            last = None
                            for mi in range(4):
                                w = T[sn][:, co + mi * 128:co + mi * 128
                                          + 128]
                                rh = T[srch][:, k * 512:k * 512 + 512]
                                rl = T[srcl][:, k * 512:k * 512 + 512]
                                e.matmul(pb[bb + mi][:, :], w, rh,
                                         start=(k == 0), stop=False)
                                last = e.matmul(pb[bb + mi][:, :], w, rl,
                                                start=False,
                                                stop=(k == 31))
                            return last
                        S.accum("tensor", mm_big,
                                [bw, B[srch], B[srcl]],
                                [PB[bb + mi] for mi in range(4)],
                                first=(k == 0), last=(k == 31))
                    for mi in range(4):
                        m = mb * 4 + mi
                        S.step("scalar", lambda e, bb=bb, mi=mi, m=m:
                               e.activation(
                            T["stB0"][:, :], pb[bb + mi][:, :], func,
                            scale=scl_s[:, m:m + 1]),
                            reads=[PB[bb + mi], cb], writes=[B["stB0"]])
                        S.step("scalar", lambda e, m=m: e.copy(
                            T[dsth][:, m * 512:m * 512 + 512],
                            T["stB0"][:, :]),
                            reads=[B["stB0"]], writes=[B[dsth]])
                        S.step("vector", lambda e, m=m: e.tensor_tensor(
                            T[dstl][:, m * 512:m * 512 + 512],
                            T["stB0"][:, :],
                            T[dsth][:, m * 512:m * 512 + 512],
                            ALU.subtract),
                            reads=[B["stB0"], B[dsth]], writes=[B[dstl]])

            big_mat("h1t", "hh", "hl", "g1h", "g1l", sh1_s, AF.Gelu, 32)
            emit_dbg("g1h", t)
            emit_dbg("g1l", t)
            big_mat("h2t", "g1h", "g1l", "hh", "hl", sh2_s, AF.Gelu, 32)
            big_mat("w3t", "hh", "hl", "y2h", "y2l", s3_s, AF.Identity, 8)
            emit_dbg("y2h", t)
            emit_dbg("y2l", t)

            stats_from(src_from_pair("y2h", "y2l", "stB0"),
                       "rbB2", "bbB2", "stB1", "stB2", "stg5", "stB1")

            def out_c(c, tc=tc):
                S.step("sync", lambda e, c=c, tc=tc: e.dma_start(
                    out=out_hbm[c * 128:(c + 1) * 128, tc],
                    in_=T["x_sb"][:, c * 512:c * 512 + 512]),
                    reads=[XCB[c]], writes=[B["out"]], sem="sDo")
            affine_residual_pair("y2h", "y2l", "stB0", "stB1",
                                 "rbB2", "bbB2", store_cb=out_c)

        S.steps.append(("sync", {sm: S.count[sm] for sm in DMA_SEMS
                                 if S.count[sm] > 0}, None, 0, None))

        per_eng = {e: [] for e in ENGS}
        for e_name, waits, fn, amt, stsem in S.steps:
            per_eng[e_name].append((waits, fn, amt, stsem))

        def make_section(eng_name):
            def section(eng):
                hw = {}
                for waits, fn, amt, stsem in per_eng[eng_name]:
                    for s in sorted(waits):
                        v = waits[s]
                        if hw.get(s, 0) < v:
                            eng.wait_ge(semh[s], v)
                            hw[s] = v
                    if fn is None:
                        continue
                    res = fn(eng)
                    insts = res if isinstance(res, list) else [res]
                    for inst in insts:
                        inst.then_inc(semh[stsem], amt)
            return section

        block.sync(make_section("sync"))
        block.gpsimd(make_section("gpsimd"))
        block.tensor(make_section("tensor"))
        block.vector(make_section("vector"))
        block.scalar(make_section("scalar"))

    return nc


def make_shared(inp):
    f32 = np.float32
    shared = {
        "wq": np.ascontiguousarray(inp["L0_attn_Wq"], dtype=f32),
        "wkv": np.ascontiguousarray(np.concatenate(
            [np.asarray(inp["L0_attn_Wk"], f32),
             np.asarray(inp["L0_attn_Wv"], f32)], axis=1)),
        "wo": np.ascontiguousarray(inp["L0_attn_Wo"], dtype=f32),
        "bq_r": np.ascontiguousarray(
            np.asarray(inp["L0_attn_bq"], f32).reshape(8, 128).T),
        "bk_r": np.ascontiguousarray(
            np.asarray(inp["L0_attn_bk"], f32).reshape(2, 128).T),
        "bv_r": np.ascontiguousarray(
            np.asarray(inp["L0_attn_bv"], f32).reshape(2, 128).T),
        "bo_r": np.ascontiguousarray(
            np.asarray(inp["L0_attn_bo"], f32).reshape(8, 128).T),
        "w12": np.ascontiguousarray(np.concatenate(
            [np.asarray(inp["L0_ffn_w1_w"], f32).T.reshape(1024, 8, 512),
             np.asarray(inp["L0_ffn_w2_w"], f32).T.reshape(1024, 8, 512)],
            axis=2).reshape(1024, 8192)),
        "h1t": np.ascontiguousarray(np.asarray(inp["L0_ffn_h1_w"], f32).T),
        "h2t": np.ascontiguousarray(np.asarray(inp["L0_ffn_h2_w"], f32).T),
        "w3t": np.ascontiguousarray(np.asarray(inp["L0_ffn_w3_w"], f32).T),
        "s1_r": np.ascontiguousarray(
            np.asarray(inp["L0_ffn_w1_s"], f32).reshape(32, 128).T),
        "s2_r": np.ascontiguousarray(
            np.asarray(inp["L0_ffn_w2_s"], f32).reshape(32, 128).T),
        "sh1_r": np.ascontiguousarray(
            np.asarray(inp["L0_ffn_h1_s"], f32).reshape(32, 128).T),
        "sh2_r": np.ascontiguousarray(
            np.asarray(inp["L0_ffn_h2_s"], f32).reshape(32, 128).T),
        "s3_r": np.ascontiguousarray(
            np.asarray(inp["L0_ffn_w3_s"], f32).reshape(8, 128).T),
    }
    invf = 1.0 / (10000.0 ** (np.arange(32, dtype=np.float64) / 32.0))
    pos = np.arange(2048, dtype=np.float64)
    ang = invf[(np.arange(128) % 32)][:, None] * pos[None, :]
    c64, s64 = np.cos(ang), np.sin(ang)
    ch = c64.astype(f32); cl = (c64 - ch).astype(f32)
    sh = s64.astype(f32); sl = (s64 - sh).astype(f32)
    shared.update({"cosh": ch, "cosl": cl, "sinh": sh, "sinl": sl})
    shared["canv"] = (np.arange(128)[:, None]
                      > (np.arange(896)[None, :] - 384)).astype(f32)
    shared["ident"] = np.eye(128, dtype=f32)
    return shared


def kernel(**inp):
    f32 = np.float32
    x = np.asarray(inp["L0_attn_x"], dtype=f32)
    xT = np.ascontiguousarray(np.transpose(x, (0, 2, 1)))
    shared = make_shared(inp)
    nc = build_program()
    in_maps = [dict(shared, xT=np.ascontiguousarray(xT[b]))
               for b in range(8)]
    res = run_bass_kernel_spmd(nc, in_maps, core_ids=list(range(8)))
    out = np.stack([np.asarray(res.results[b]["out"]).T for b in range(8)])
    return np.ascontiguousarray(out.astype(np.float32))
